# version 06
# baseline (speedup 1.0000x reference)
import numpy as np
import ml_dtypes
import concourse.bass as bass
import concourse.mybir as mybir
from concourse.bass_utils import run_bass_kernel_spmd

F32 = mybir.dt.float32
BF16 = mybir.dt.bfloat16
FP16 = mybir.dt.float16
I32 = mybir.dt.int32
AF = mybir.ActivationFunctionType
ALU = mybir.AluOpType
AX = mybir.AxisListType

C = 1024
T = 4096
CT = 8
FF = 4096
FT = 32
II = 2816
IT = 22
DC = 512
DI = 1024
NS = 16
EPS = 1e-5
QMAXR = 1.0 / 127.0

QA, QB, QC, QD, QE, QF1, QF2 = 512, 256, 512, 512, 512, 512, 256

THS = (0.24419428096993973, 0.4639691338428854, 0.6861859295255306,
       0.9059607823984763, 1.0839784132255623)

_PHASES = ("A", "B", "C1", "C2", "D", "E", "F1", "F2")
_DEBUG_OUT = ()
_D_TRUNC = 0
_D_NIT = None
_F2_TRUNC = 0
_DBG_G = False

ENGINES = ("tensor", "vector", "scalar", "gpsimd", "sync")


class Sched:
    NDSEM = 8

    def __init__(self, nc):
        self.nc = nc
        self.ops = {e: [] for e in ENGINES}
        self.last_writer = {}
        self.readers = {}
        self.barrier = []
        self.dma_ord = {e: 0 for e in ENGINES}
        self.last_dma = {}

    def emit(self, eng, fn, deps=(), reads=(), writes=(), is_dma=False):
        idx = len(self.ops[eng])
        tok = (eng, idx)
        alldeps = list(deps) + list(self.barrier)
        for k in reads:
            w = self.last_writer.get(k)
            if w is not None:
                alldeps.append(w)
        for k in writes:
            w = self.last_writer.get(k)
            if w is not None:
                alldeps.append(w)
            alldeps.extend(self.readers.get(k, ()))
        seen = set()
        dd = []
        for d in alldeps:
            if d != tok and d not in seen:
                seen.add(d)
                dd.append(d)
        op = dict(fn=fn, deps=dd, is_dma=is_dma, inc=False)
        if is_dma:
            k = self.dma_ord[eng]
            self.dma_ord[eng] = k + 1
            op["dsem"] = k % self.NDSEM
            op["dcnt"] = k // self.NDSEM
            self.last_dma[(eng, op["dsem"])] = tok
        self.ops[eng].append(op)
        for k in reads:
            self.readers.setdefault(k, []).append(tok)
        for k in writes:
            self.last_writer[k] = tok
            self.readers[k] = []
        return tok

    def dma(self, eng, out, in_, reads=(), writes=(), deps=()):
        return self.emit(eng, (lambda e, o=out, i=in_: e.dma_start(out=o, in_=i)),
                         deps=deps, reads=reads, writes=writes, is_dma=True)

    def barrier_like(self, keys):
        toks = []
        for k in keys:
            w = self.last_writer.get(k)
            if w is not None:
                toks.append(w)
        return toks

    def set_barrier(self):
        toks = []
        for e in ENGINES:
            if self.ops[e]:
                toks.append((e, len(self.ops[e]) - 1))
        for t in self.last_dma.values():
            if t not in toks:
                toks.append(t)
        self.barrier = toks

    def _semkey(self, tok):
        (pe, pi) = tok
        p = self.ops[pe][pi]
        if p["is_dma"]:
            return (pe, "d", p["dsem"]), 16 * (p["dcnt"] + 1)
        return (pe, "c"), self.cnt_c[pe][pi + 1]

    def materialize(self):
        nc = self.nc
        for e in ENGINES:
            for op in self.ops[e]:
                if op["is_dma"]:
                    op["inc"] = True
                for (pe, pi) in op["deps"]:
                    self.ops[pe][pi]["inc"] = True
        self.cnt_c = {e: [0] * (len(self.ops[e]) + 1) for e in ENGINES}
        for e in ENGINES:
            cc = 0
            for i, op in enumerate(self.ops[e]):
                if op["inc"] and not op["is_dma"]:
                    cc += 1
                self.cnt_c[e][i + 1] = cc
        self._sem_ctx = []
        self.sems = {}
        for e in ENGINES:
            if self.cnt_c[e][-1] > 0:
                s = nc.semaphore(f"sem_{e}_c")
                self.sems[(e, "c")] = s.__enter__()
                self._sem_ctx.append(s)
            for si in range(self.NDSEM):
                if (e, si) in self.last_dma:
                    s = nc.semaphore(f"sem_{e}_d{si}")
                    self.sems[(e, "d", si)] = s.__enter__()
                    self._sem_ctx.append(s)

    def run_engine(self, e, eng_obj):
        waited = {}
        for op in self.ops[e]:
            needs = {}
            for tok in op["deps"]:
                key, v = self._semkey(tok)
                if v > needs.get(key, 0):
                    needs[key] = v
            if op["is_dma"] and op["dcnt"] > 0:
                key = (e, "d", op["dsem"])
                v = 16 * op["dcnt"]
                if v > needs.get(key, 0):
                    needs[key] = v
            for key, v in sorted(needs.items()):
                if v > waited.get(key, 0):
                    eng_obj.wait_ge(self.sems[key], v)
                    waited[key] = v
            instr = op["fn"](eng_obj)
            if op["inc"]:
                if op["is_dma"]:
                    instr.then_inc(self.sems[(e, "d", op["dsem"])], 16)
                else:
                    instr.then_inc(self.sems[(e, "c")], 1)

    def close(self):
        for s in reversed(self._sem_ctx):
            s.__exit__(None, None, None)


def _build():
    nc = bass.Bass("TRN2", target_bir_lowering=False)
    sch = Sched(nc)
    E, D = sch.emit, sch.dma

    def din(name, shape, dtype=F32):
        return nc.dram_tensor(name, list(shape), dtype, kind="ExternalInput").ap()

    def dscr(name, shape, dtype):
        kind = "ExternalOutput" if name in _DEBUG_OUT else "Internal"
        return nc.dram_tensor(name, list(shape), dtype, kind=kind).ap()

    x_d = din("x", (C, T))
    pcv_d = din("pcvec", (128, 26 * 8))
    alog_d = din("Alog", (128, 256))
    convw_d = din("convw", (128, 64))
    clips_d = din("clips", (2,))
    ident_d = din("ident", (128, 128), FP16)
    sel_d = din("sel", (64, 32 * 128), FP16)
    wk_d = din("Wk", (C, C)); wv_d = din("Wv", (C, C))
    wr_d = din("Wr", (C, C)); wo_d = din("Wo", (C, C))
    wck_d = din("Wck", (C, FF)); wcv_d = din("Wcv", (FF, C)); wcr_d = din("Wcr", (C, C))
    wg_d = din("wg_n", (II, C)); wu_d = din("wu_n", (II, C)); wd_d = din("wd_n", (C, II))
    inw_d = din("inwT", (2, DC, 2 * DI))
    xpw_d = din("xpwT", (2, DI, 64))
    dtw_d = din("dtwT", (2, 32, DI))
    mow_d = din("mowT", (2, DI, DC))
    pow_d = din("powT", (C, C))
    wfc_d = din("wfcT", (C, FF))
    wpr_d = din("wprT", (FF, C))

    out_d = nc.dram_tensor("out", [C, T], F32, kind="ExternalOutput").ap()

    xn_d = dscr("xn_s", (C, T), FP16)
    xmid_d = dscr("xmid_s", (C, T), FP16)
    s_d = dscr("s_s", (C, T), FP16)
    rr_d = dscr("rr_s", (C, T), FP16)
    rsq_d = dscr("rsq_s", (FF, T), FP16)
    gu_d = dscr("gu_s", (II, T), FP16)
    x1_d = dscr("x1_s", (C, T), FP16)
    s1_d = dscr("s1_s", (C, T), FP16)
    stats_d = dscr("stats_s", (16, T), F32)
    amax_d = dscr("amax_s", (128, 8), F32)
    cc_in1 = dscr("ccin1", (1,), F32)
    cc_out1 = dscr("ccout1", (1,), F32)
    cc_in2 = dscr("ccin2", (1,), F32)
    cc_out2 = dscr("ccout2", (1,), F32)

    def dview(ap, p=128):
        return ap.rearrange("(a p) t -> p a t", p=p)

    x_v = dview(x_d); xn_v = dview(xn_d); xmid_v = dview(xmid_d)
    s_v = dview(s_d); rr_v = dview(rr_d); rsq_v = dview(rsq_d)
    gu_v = dview(gu_d); x1_v = dview(x1_d); s1_v = dview(s1_d)
    out_v = dview(out_d)

    R_M, R_R0, R_R1, R_M1, R_R01, R_MF, R_R0F, R_RFC = 0, 1, 2, 3, 4, 5, 6, 7

    psum_ctx = []
    def psum(name, shape, dtype=F32):
        c = nc.psum_tensor(name, list(shape), dtype)
        t = c.__enter__()
        psum_ctx.append(c)
        return t
    PS = [psum(f"ps{i}", (128, 512)) for i in range(6)]
    pstat = psum("pstat", (128, 512))
    psT = psum("psT", (128, 1024), FP16)

    per = nc.auto_arena()
    pcv = per.sbuf_tensor("pcv", [128, 26 * 8], F32)
    alog = per.sbuf_tensor("alog_t", [128, 256], F32)
    aneg = per.sbuf_tensor("aneg", [128, 256], F32)
    convw = per.sbuf_tensor("convw_t", [128, 64], F32)
    ew_t = per.sbuf_tensor("ew_t", [128, 8], F32)
    eu_t = per.sbuf_tensor("eu_t", [128, 8], F32)
    ident = per.sbuf_tensor("ident_t", [128, 128], FP16)
    onesb = per.sbuf_tensor("onesb", [128, 1], FP16)
    ones1f = per.sbuf_tensor("ones1f", [1, 128], F32)
    srow = per.sbuf_tensor("srow", [1, 2048], F32)
    clp = per.sbuf_tensor("clp", [1, 2], F32)
    dcol = per.sbuf_tensor("dcol", [128, 1], F32)
    sc1 = per.sbuf_tensor("sc1", [128, 2], F32)
    sc2 = per.sbuf_tensor("sc2", [128, 2], F32)
    amaxa = per.sbuf_tensor("amaxa", [128, 8], F32)
    t15 = per.sbuf_tensor("t15", [128, 1], F32)
    i32d = per.sbuf_tensor("i32d", [128, 1], I32)
    tiny = per.sbuf_tensor("tinyws", [1, 16], F32)
    cbias = per.sbuf_tensor("cbias", [128, 4], F32)
    per.allocate()

    PC = dict(tm_k=0, tm_v=1, tm_r=2, td=3, tf=4, g0=5, b0=6, g1=7, b1=8,
              cm_k=9, cm_r=10, gL=11, bL=12, po_b=13, cb0=14, cb1=15,
              dtb0=16, dtb1=17, Dp0=18, Dp1=19, rms0=20, rms1=21)

    def pcol(slot, ct):
        return pcv[:, PC[slot] * 8 + ct: PC[slot] * 8 + ct + 1]

    def mm(ps_ap, lhsT, rhs, start, stop, reads, writes, deps=()):
        return E("tensor", (lambda e, o=ps_ap, l=lhsT, r=rhs, s1=start, s2=stop:
                            e.matmul(out=o, lhsT=l, rhs=r, start=s1, stop=s2)),
                 deps=deps, reads=reads, writes=writes)

    def act(out, in_, func, reads, writes, scale=None, bias=None, deps=()):
        kw = {}
        if scale is not None:
            kw["scale"] = scale
        if bias is not None:
            kw["bias"] = bias
        return E("scalar", (lambda e, o=out, i=in_, f=func, k=dict(kw):
                            e.activation(out=o, in_=i, func=f, **k)),
                 deps=deps, reads=reads, writes=writes)

    def tt(out, a, b, op, reads, writes, eng="vector", deps=()):
        return E(eng, (lambda e, o=out, x=a, y=b, p=op:
                       e.tensor_tensor(out=o, in0=x, in1=y, op=p)),
                 deps=deps, reads=reads, writes=writes)

    def ts(out, a, s1, s2, op0, op1, reads, writes, eng="vector", deps=()):
        return E(eng, (lambda e, o=out, x=a, u=s1, v=s2, p0=op0, p1=op1:
                       e.tensor_scalar(out=o, in0=x, scalar1=u, scalar2=v, op0=p0, op1=p1)),
                 deps=deps, reads=reads, writes=writes)

    def stt(out, in0, scalar, in1, op0, op1, reads, writes, deps=()):
        return E("vector", (lambda e, o=out, x=in0, s=scalar, y=in1, p0=op0, p1=op1:
                            e.scalar_tensor_tensor(out=o, in0=x, scalar=s, in1=y,
                                                   op0=p0, op1=p1)),
                 deps=deps, reads=reads, writes=writes)

    def scan(out, d0, d1, initial, reads, writes, deps=()):
        return E("vector", (lambda e, o=out, a=d0, b=d1, i=initial:
                            e.tensor_tensor_scan(out=o, data0=a, data1=b, initial=i,
                                                 op0=ALU.mult, op1=ALU.add)),
                 deps=deps, reads=reads, writes=writes)

    def vcopy(out, in_, reads, writes, eng="vector", deps=()):
        if eng == "scalar":
            return E(eng, (lambda e, o=out, i=in_: e.copy(out=o, in_=i)),
                     deps=deps, reads=reads, writes=writes)
        return E(eng, (lambda e, o=out, i=in_: e.tensor_copy(out=o, in_=i)),
                 deps=deps, reads=reads, writes=writes)

    def redmax(out, in_, reads, writes, deps=()):
        return E("vector", (lambda e, o=out, i=in_:
                            e.reduce_max(out=o, in_=i, axis=AX.X, apply_absolute_value=True)),
                 deps=deps, reads=reads, writes=writes)

    def memset(ap, val, writes, eng="gpsimd", deps=()):
        return E(eng, (lambda e, o=ap, v=val: e.memset(o, v)), deps=deps, writes=writes)

    def transpose(ps_ap, in_ap, reads, writes, deps=()):
        return E("tensor", (lambda e, o=ps_ap, i=in_ap:
                            e.transpose(out=o, in_=i, identity=ident[:, :])),
                 deps=deps, reads=list(reads) + ["ident"], writes=writes)

    def bcast_row(ps_ap, row_ap, writes, deps=(), reads=()):
        return E("tensor", (lambda e, o=ps_ap, r=row_ap:
                            e.matmul(out=o, lhsT=ones1f[:, :], rhs=r, start=True, stop=True)),
                 deps=deps, reads=list(reads) + ["srow", "ones1f"], writes=writes)

    def smul(out, in_, mulc, reads, writes):
        return E("scalar", (lambda e, o=out, i=in_, m=mulc: e.mul(out=o, in_=i, mul=m)),
                 reads=reads, writes=writes)

    D("sync", pcv[:, :], pcv_d, writes=["pcv"])
    D("sync", alog[:, :], alog_d, writes=["alog"])
    D("sync", convw[:, :], convw_d, writes=["convw"])
    D("sync", clp[0:1, 0:2], clips_d.rearrange("(a b) -> a b", a=1), writes=["clp"])
    D("sync", ident[:, :], ident_d, writes=["ident"])
    memset(cbias[:, 0:1], EPS, writes=["cbias"])
    memset(cbias[:, 1:2], 512.0, writes=["cbias"])
    memset(cbias[:, 2:3], 512.5, writes=["cbias"])
    memset(cbias[:, 3:4], 1.0, writes=["cbias"])
    memset(onesb[:, :], 1.0, writes=["onesb"])
    memset(amaxa[:, :], 0.0, writes=["amaxa"])
    memset(ones1f[:, :], 1.0, writes=["ones1f"])
    memset(t15[:, :], 1.5, writes=["t15"])
    act(i32d[:, :], t15[:, :], AF.Identity, reads=["t15"], writes=["i32d"])
    ts(dcol[:, :], i32d[:, :], 1.0, None, ALU.subtract, ALU.bypass,
       reads=["i32d"], writes=["dcol"])
    act(ew_t[:, :], pcv[:, PC["td"] * 8:PC["td"] * 8 + 8], AF.Exp, reads=["pcv"], writes=["ew"])
    act(ew_t[:, :], ew_t[:, :], AF.Exp, scale=-1.0, reads=["ew"], writes=["ew"])
    act(eu_t[:, :], pcv[:, PC["tf"] * 8:PC["tf"] * 8 + 8], AF.Exp, reads=["pcv"], writes=["eu"])
    act(aneg[:, :], alog[:, :], AF.Exp, reads=["alog"], writes=["aneg"])
    ts(aneg[:, :], aneg[:, :], -1.0, None, ALU.mult, ALU.bypass, reads=["aneg"], writes=["aneg"])

    def quant(out_bf, in_ap, iscale, i32a, i32b, tmpa, tmpb, kin, kout, ktmp):
        act(i32a, in_ap, AF.Identity, scale=iscale, bias=cbias[:, 1:2],
            reads=list(kin) + ["cbias"], writes=[ktmp + "a"])
        act(i32b, in_ap, AF.Identity, scale=iscale, bias=cbias[:, 2:3],
            reads=list(kin) + ["cbias"], writes=[ktmp + "b"])
        tt(tmpa, i32a, i32b, ALU.subtract, reads=[ktmp + "a", ktmp + "b"],
           writes=[ktmp + "t"])
        stt(tmpb, tmpa, dcol[:, 0:1], i32b, ALU.mult, ALU.add,
            reads=[ktmp + "t", ktmp + "b", "dcol"], writes=[ktmp + "d"])
        ts(tmpa, tmpb, 384.0, 639.0, ALU.max, ALU.min, reads=[ktmp + "d"],
           writes=[ktmp + "t"])
        ts(out_bf, tmpa, 512.0, None, ALU.subtract, ALU.bypass,
           reads=[ktmp + "t"], writes=[kout])

    def stats_rows(xb, sq, q, t0, kxb, ksq, rows, nch=C, nt=CT):
        for ct in range(nt):
            mm(pstat[0:1, 0:q], onesb[:, :], xb[:, ct, :], ct == 0, ct == nt - 1,
               reads=[kxb, "onesb"], writes=["pstat"])
        smul(srow[0:1, 0:q], pstat[0:1, 0:q], 1.0 / nch, reads=["pstat"], writes=["srow"])
        for ct in range(nt):
            mm(pstat[0:1, 0:q], onesb[:, :], sq[:, ct, :], ct == 0, ct == nt - 1,
               reads=[ksq, "onesb"], writes=["pstat"])
        act(srow[0:1, q:2 * q], srow[0:1, 0:q], AF.Square, reads=["srow"], writes=["srow"])
        stt(srow[0:1, q:2 * q], pstat[0:1, 0:q], 1.0 / nch, srow[0:1, q:2 * q],
            ALU.mult, ALU.subtract, reads=["pstat", "srow"], writes=["srow"])
        act(srow[0:1, 2 * q:3 * q], srow[0:1, q:2 * q], AF.Ln, bias=cbias[0:1, 0:1],
            reads=["srow", "cbias"], writes=["srow"])
        act(srow[0:1, 2 * q:3 * q], srow[0:1, 2 * q:3 * q], AF.Exp, scale=-0.5,
            reads=["srow"], writes=["srow"])
        for (rid, kind) in rows:
            if kind == "m":
                src = srow[0:1, 0:q]
            elif kind == "r0":
                src = srow[0:1, 2 * q:3 * q]
            else:
                act(srow[0:1, 3 * q:4 * q], srow[0:1, 2 * q:3 * q], AF.Square,
                    reads=["srow"], writes=["srow"])
                tt(srow[0:1, 3 * q:4 * q], srow[0:1, q:2 * q], srow[0:1, 3 * q:4 * q],
                   ALU.mult, reads=["srow"], writes=["srow"])
                act(srow[0:1, 3 * q:4 * q], srow[0:1, 3 * q:4 * q], AF.Ln,
                    bias=cbias[0:1, 0:1], reads=["srow", "cbias"], writes=["srow"])
                act(srow[0:1, 3 * q:4 * q], srow[0:1, 3 * q:4 * q], AF.Exp, scale=-0.5,
                    reads=["srow"], writes=["srow"])
                if kind == "rfc":
                    tt(srow[0:1, 3 * q:4 * q], srow[0:1, 3 * q:4 * q],
                       srow[0:1, 2 * q:3 * q], ALU.mult, reads=["srow"], writes=["srow"])
                src = srow[0:1, 3 * q:4 * q]
            D("sync", stats_d[rid:rid + 1, t0:t0 + q], src, reads=["srow"])

    def scale_pipeline(ccin, ccout, scol, clip_idx, skey):
        D("sync", amax_d, amaxa[:, :], reads=["amaxa"], writes=["amaxd"])
        D("sync", srow[0:1, 0:1024],
          amax_d.rearrange("p a -> (p a)").rearrange("(a b) -> a b", a=1),
          reads=["amaxd"], writes=["srow"])
        redmax(tiny[0:1, 0:1], srow[0:1, 0:1024], reads=["srow"], writes=["tiny"])
        D("gpsimd", ccin.rearrange("(a b) -> a b", a=1)[0:1, 0:1], tiny[0:1, 0:1],
          reads=["tiny"], writes=["ccin"])
        E("gpsimd", (lambda e, ci=ccin, co=ccout: e.collective_compute(
            "AllReduce", ALU.max, replica_groups=[list(range(8))],
            ins=[ci.opt()], outs=[co.opt()])),
          reads=["ccin"], writes=["ccbuf"])
        D("gpsimd", tiny[0:1, 1:2], ccout.rearrange("(a b) -> a b", a=1)[0:1, 0:1],
          reads=["ccbuf"], writes=["tiny"])
        E("scalar", lambda e, ci=clip_idx: e.activation(
            out=tiny[0:1, 2:3], in_=clp[0:1, ci:ci + 1], func=AF.Sigmoid),
          reads=["clp", "tiny"], writes=["tiny"])
        tt(tiny[0:1, 3:4], tiny[0:1, 1:2], tiny[0:1, 2:3], ALU.mult,
           reads=["tiny"], writes=["tiny"])
        ts(tiny[0:1, 3:4], tiny[0:1, 3:4], QMAXR, None, ALU.mult, ALU.bypass,
           reads=["tiny"], writes=["tiny"])
        act(tiny[0:1, 4:5], tiny[0:1, 3:4], AF.Ln, reads=["tiny"], writes=["tiny"])
        act(tiny[0:1, 4:5], tiny[0:1, 4:5], AF.Exp, scale=-1.0, reads=["tiny"], writes=["tiny"])
        E("tensor", (lambda e: e.matmul(out=PS[5][:, 0:2], lhsT=ones1f[:, :],
                                        rhs=tiny[0:1, 3:5], start=True, stop=True)),
          reads=["tiny", "ones1f"], writes=["ps5"])
        vcopy(scol[:, 0:2], PS[5][:, 0:2], reads=["ps5"], writes=[skey])

    def load_w_bf16(dst, dst_key, dram_view, kt_n, cols, stage, kstage, seg=1024):
        seg = min(cols, seg)
        for kt in range(kt_n):
            for s0 in range(0, cols, seg):
                D("gpsimd", stage[:, 0:seg], dram_view[:, kt, s0:s0 + seg], writes=[kstage])
                vcopy(dst[:, kt, s0:s0 + seg], stage[:, 0:seg], reads=[kstage],
                      writes=[dst_key], eng="gpsimd")

    if "A" in _PHASES:
        ar = nc.auto_arena()
        xch = ar.sbuf_tensor("A_x", [128, CT, QA], F32)
        xb = ar.sbuf_tensor("A_xb", [128, CT, QA], FP16)
        sq = ar.sbuf_tensor("A_sq", [128, CT, QA], FP16)
        xnb = ar.sbuf_tensor("A_xn", [128, CT, QA], FP16)
        amx = ar.sbuf_tensor("A_amx", [128, 1], F32)
        ar.allocate()
        for ci in range(T // QA):
            t0 = ci * QA
            D("sync", xch[:, :, :], x_v[:, :, t0:t0 + QA], writes=["A_x"])
            for ct in range(CT):
                act(xb[:, ct, :], xch[:, ct, :], AF.Identity, reads=["A_x"], writes=["A_xb"])
                act(sq[:, ct, :], xch[:, ct, :], AF.Square, reads=["A_x"], writes=["A_sq"])
            stats_rows(xb, sq, QA, t0, "A_xb", "A_sq",
                       [(R_M, "m"), (R_R0, "r0"), (R_R1, "r1")])
            bcast_row(PS[0][:, 0:QA], srow[0:1, 0:QA], writes=["ps0"])
            bcast_row(PS[1][:, 0:QA], srow[0:1, 2 * QA:3 * QA], writes=["ps1"])
            for ct in range(CT):
                tt(xch[:, ct, :], xch[:, ct, :], PS[0][:, 0:QA], ALU.subtract,
                   reads=["A_x", "ps0"], writes=["A_x"])
                tt(xnb[:, ct, :], xch[:, ct, :], PS[1][:, 0:QA], ALU.mult,
                   reads=["A_x", "ps1"], writes=["A_xn"])
                redmax(amx[:, 0:1], xnb[:, ct, :], reads=["A_xn"], writes=["A_amx"])
                if ci == 0 and ct == 0:
                    vcopy(amaxa[:, 0:1], amx[:, 0:1], reads=["A_amx"], writes=["amaxa"])
                else:
                    tt(amaxa[:, 0:1], amaxa[:, 0:1], amx[:, 0:1], ALU.max,
                       reads=["amaxa", "A_amx"], writes=["amaxa"])
            D("sync", xn_v[:, :, t0:t0 + QA], xnb[:, :, :], reads=["A_xn"])
        scale_pipeline(cc_in1, cc_out1, sc1, 0, "sxcol")
        sch.set_barrier()
        ar.free()

    if "B" in _PHASES:
        ar = nc.auto_arena()
        wK = ar.sbuf_tensor("B_wk", [128, CT, C], FP16)
        wV = ar.sbuf_tensor("B_wv", [128, CT, C], FP16)
        wR = ar.sbuf_tensor("B_wr", [128, CT, C], FP16)
        wO = ar.sbuf_tensor("B_wo", [128, CT, C], FP16)
        stg = ar.sbuf_tensor("B_stg", [128, 512], F32)
        xnb = ar.sbuf_tensor("B_xn", [128, 3, CT, QB], FP16)
        xah = ar.sbuf_tensor("B_xa", [128, CT, QB + 1], F32)
        dt_ = ar.sbuf_tensor("B_d", [128, CT, QB], FP16)
        mixR = ar.sbuf_tensor("B_mixR", [128, CT, QB], FP16)
        mixK = ar.sbuf_tensor("B_mixK", [128, CT, QB], FP16)
        mixV = ar.sbuf_tensor("B_mixV", [128, CT, QB], FP16)
        Et = ar.sbuf_tensor("B_E", [128, 2, CT, QB], F32)
        EVt = ar.sbuf_tensor("B_EV", [128, CT, QB], F32)
        NUMt = ar.sbuf_tensor("B_NUM", [128, CT, QB], F32)
        Ah = ar.sbuf_tensor("B_Ah", [128, CT, QB + 1], F32)
        Bh = ar.sbuf_tensor("B_Bh", [128, CT, QB + 1], F32)
        vps = ar.sbuf_tensor("B_vps", [128, 2, QB], F32)
        rsig = ar.sbuf_tensor("B_rs", [128, 2, CT, QB], FP16)
        rw = ar.sbuf_tensor("B_rw", [128, 2, CT, QB], FP16)
        xm = ar.sbuf_tensor("B_xm", [128, 2, QB], F32)
        xm16 = ar.sbuf_tensor("B_xm16", [128, CT, QB], FP16)
        sq16 = ar.sbuf_tensor("B_sq16", [128, CT, QB], FP16)
        ewb = ar.sbuf_tensor("B_ewb", [128, CT, QB], F32)
        ar.allocate()

        load_w_bf16(wK, "B_wk", dview(wk_d), CT, C, stg, "B_stg", seg=512)
        load_w_bf16(wV, "B_wv", dview(wv_d), CT, C, stg, "B_stg", seg=512)
        load_w_bf16(wR, "B_wr", dview(wr_d), CT, C, stg, "B_stg", seg=512)
        load_w_bf16(wO, "B_wo", dview(wo_d), CT, C, stg, "B_stg", seg=512)
        memset(ewb[:, :, :], 1.0, writes=["B_ewb"])
        for ct in range(CT):
            ts(ewb[:, ct, :], ewb[:, ct, :], ew_t[:, ct:ct + 1], None, ALU.mult, ALU.bypass,
               reads=["B_ewb", "ew"], writes=["B_ewb"])
        memset(xah[:, :, 0:1], 0.0, writes=["B_xa"])
        memset(Ah[:, :, 0:1], 0.0, writes=["B_Ah"])
        memset(Bh[:, :, 0:1], 0.0, writes=["B_Bh"])

        NCH = T // QB

        def b_load(ci):
            p = ci % 2
            p3 = ci % 3
            t0 = ci * QB
            D("sync", xnb[:, p3, :, :], xn_v[:, :, t0:t0 + QB], writes=[f"B_xn{p3}"])
            D("sync", srow[0:1, 1024 + p * QB:1024 + p * QB + QB],
              stats_d[R_R1:R_R1 + 1, t0:t0 + QB], writes=[f"B_srow{p}"])

        def b_tail(cj):
            p = cj % 2
            p3 = cj % 3
            t0 = cj * QB
            for ot in range(CT):
                q = ot % 2
                pb = PS[2 + q]
                for ct in range(CT):
                    mm(pb[:, 0:QB], wO[:, ct, ot * 128:(ot + 1) * 128], rw[:, p, ct, :],
                       ct == 0, ct == CT - 1, reads=["B_wo", f"B_rw{p}"],
                       writes=[f"pso{q}"])
                tt(xm[:, q, :], xnb[:, p3, ot, :], pb[:, 0:QB], ALU.add,
                   reads=[f"B_xn{p3}", f"pso{q}"], writes=[f"B_xm{q}"])
                act(xm16[:, ot, :], xm[:, q, :], AF.Identity, reads=[f"B_xm{q}"],
                    writes=["B_xm16"])
                act(sq16[:, ot, :], xm[:, q, :], AF.Square, reads=[f"B_xm{q}"],
                    writes=["B_sq16"])
            stats_rows(xm16, sq16, QB, t0, "B_xm16", "B_sq16",
                       [(R_M1, "m"), (R_R01, "r0")])
            D("gpsimd", xmid_v[:, :, t0:t0 + QB], xm16[:, :, :], reads=["B_xm16"])

        b_load(0)
        b_load(1)
        for ci in range(NCH):
            p = ci % 2
            p3 = ci % 3
            E("tensor", (lambda e, p=p: e.matmul(
                out=PS[5][:, 256:512], lhsT=ones1f[:, :],
                rhs=srow[0:1, 1024 + p * QB:1024 + p * QB + QB], start=True, stop=True)),
              reads=[f"B_srow{p}", "ones1f"], writes=["psbc", "psv1"])
            for ct in range(CT):
                tt(xah[:, ct, 1:QB + 1], xnb[:, p3, ct, :], PS[5][:, 256:512],
                   ALU.mult, reads=[f"B_xn{p3}", "psbc"], writes=["B_xa"])
                ts(xah[:, ct, 1:QB + 1], xah[:, ct, 1:QB + 1], pcol("g0", ct), pcol("b0", ct),
                   ALU.mult, ALU.add, reads=["B_xa", "pcv"], writes=["B_xa"])
                tt(dt_[:, ct, :], xah[:, ct, 1:QB + 1], xah[:, ct, 0:QB], ALU.subtract,
                   reads=["B_xa"], writes=["B_d"])
                stt(mixR[:, ct, :], dt_[:, ct, :], pcol("tm_r", ct), xah[:, ct, 0:QB],
                    ALU.mult, ALU.add, reads=["B_d", "B_xa", "pcv"], writes=["B_mixR"])
                stt(mixK[:, ct, :], dt_[:, ct, :], pcol("tm_k", ct), xah[:, ct, 0:QB],
                    ALU.mult, ALU.add, reads=["B_d", "B_xa", "pcv"], writes=["B_mixK"])
                stt(mixV[:, ct, :], dt_[:, ct, :], pcol("tm_v", ct), xah[:, ct, 0:QB],
                    ALU.mult, ALU.add, reads=["B_d", "B_xa", "pcv"], writes=["B_mixV"])
            vcopy(xah[:, :, 0:1], xah[:, :, QB:QB + 1], reads=["B_xa"], writes=["B_xa"])
            for ot in range(CT):
                q = ot % 2
                pb = PS[2 + q]
                for ct in range(CT):
                    mm(pb[:, 0:QB], wR[:, ct, ot * 128:(ot + 1) * 128], mixR[:, ct, :],
                       ct == 0, ct == CT - 1, reads=["B_wr", "B_mixR"], writes=[f"pso{q}"])
                act(rsig[:, p, ot, :], pb[:, 0:QB], AF.Sigmoid, reads=[f"pso{q}"],
                    writes=[f"B_rs{p}"])
            for ot in range(CT):
                q = ot % 2
                pb = PS[q]
                for ct in range(CT):
                    mm(pb[:, 0:QB], wK[:, ct, ot * 128:(ot + 1) * 128], mixK[:, ct, :],
                       ct == 0, ct == CT - 1, reads=["B_wk", "B_mixK"], writes=[f"psk{q}"])
                act(Et[:, p, ot, :], pb[:, 0:QB], AF.Exp, reads=[f"psk{q}"],
                    writes=[f"B_E{p}_{ot}"])
                scan(Bh[:, ot, 1:QB + 1], ewb[:, ot, :], Et[:, p, ot, :], Bh[:, ot, 0:1],
                     reads=["B_ewb", f"B_E{p}_{ot}", "B_Bh"], writes=["B_Bh"])
            for ot in range(CT):
                q = ot % 2
                pb = PS[4 + q]
                for ct in range(CT):
                    mm(pb[:, 0:QB], wV[:, ct, ot * 128:(ot + 1) * 128], mixV[:, ct, :],
                       ct == 0, ct == CT - 1, reads=["B_wv", "B_mixV"], writes=[f"psv{q}"])
                vcopy(vps[:, q, :], pb[:, 0:QB], reads=[f"psv{q}"], writes=[f"B_vps{q}"],
                      eng="scalar")
                tt(EVt[:, ot, :], Et[:, p, ot, :], vps[:, q, :], ALU.mult,
                   reads=[f"B_E{p}_{ot}", f"B_vps{q}"], writes=[f"B_EV{ot}"])
                act(Et[:, p, ot, :], Et[:, p, ot, :], AF.Identity, scale=eu_t[:, ot:ot + 1],
                    reads=[f"B_E{p}_{ot}", "eu"], writes=[f"B_E{p}_{ot}"])
                tt(NUMt[:, ot, :], Et[:, p, ot, :], vps[:, q, :], ALU.mult,
                   reads=[f"B_E{p}_{ot}", f"B_vps{q}"], writes=[f"B_NUM{ot}"])
                scan(Ah[:, ot, 1:QB + 1], ewb[:, ot, :], EVt[:, ot, :], Ah[:, ot, 0:1],
                     reads=["B_ewb", f"B_EV{ot}", "B_Ah"], writes=["B_Ah"])
                tt(NUMt[:, ot, :], NUMt[:, ot, :], Ah[:, ot, 0:QB], ALU.add,
                   reads=[f"B_NUM{ot}", "B_Ah"], writes=[f"B_NUM{ot}"])
                tt(EVt[:, ot, :], Bh[:, ot, 0:QB], Et[:, p, ot, :], ALU.add,
                   reads=["B_Bh", f"B_E{p}_{ot}"], writes=[f"B_EV{ot}"])
                act(EVt[:, ot, :], EVt[:, ot, :], AF.Ln, reads=[f"B_EV{ot}"],
                    writes=[f"B_EV{ot}"])
                act(EVt[:, ot, :], EVt[:, ot, :], AF.Exp, scale=-1.0,
                    reads=[f"B_EV{ot}"], writes=[f"B_EV{ot}"])
                tt(NUMt[:, ot, :], NUMt[:, ot, :], EVt[:, ot, :], ALU.mult,
                   reads=[f"B_NUM{ot}", f"B_EV{ot}"], writes=[f"B_NUM{ot}"])
                tt(rw[:, p, ot, :], NUMt[:, ot, :], rsig[:, p, ot, :], ALU.mult,
                   reads=[f"B_NUM{ot}", f"B_rs{p}"], writes=[f"B_rw{p}"])
            vcopy(Ah[:, :, 0:1], Ah[:, :, QB:QB + 1], reads=["B_Ah"], writes=["B_Ah"])
            vcopy(Bh[:, :, 0:1], Bh[:, :, QB:QB + 1], reads=["B_Bh"], writes=["B_Bh"])
            if ci > 0:
                b_tail(ci - 1)
            if ci + 2 < NCH:
                b_load(ci + 2)
        b_tail(NCH - 1)
        sch.set_barrier()
        ar.free()

    if "C1" in _PHASES:
        ar = nc.auto_arena()
        wCK = ar.sbuf_tensor("C_wck", [128, CT, FF], FP16)
        wCR = ar.sbuf_tensor("C_wcr", [128, CT, C], FP16)
        stg = ar.sbuf_tensor("C_stg", [128, 1024], F32)
        xmb = ar.sbuf_tensor("C_xm", [128, 2, CT, QC], FP16)
        xch = ar.sbuf_tensor("C_xc", [128, CT, QC + 1], F32)
        dt_ = ar.sbuf_tensor("C_d", [128, CT, QC], FP16)
        mk = ar.sbuf_tensor("C_mk", [128, 2, CT, QC], FP16)
        mr = ar.sbuf_tensor("C_mr", [128, 2, CT, QC], FP16)
        rl = ar.sbuf_tensor("C_rl", [128, 2, QC], FP16)
        rq = ar.sbuf_tensor("C_rq", [128, 2, QC], FP16)
        rrt = ar.sbuf_tensor("C_rr", [128, 2, CT, QC], FP16)
        ar.allocate()
        load_w_bf16(wCK, "C_wck", dview(wck_d), CT, FF, stg, "C_stg")
        load_w_bf16(wCR, "C_wcr", dview(wcr_d), CT, C, stg, "C_stg")
        memset(xch[:, :, 0:1], 0.0, writes=["C_xc"])
        NCH = T // QC

        def c_load(ci):
            p = ci % 2
            t0 = ci * QC
            D("sync", xmb[:, p, :, :], xmid_v[:, :, t0:t0 + QC], writes=[f"C_xm{p}"])
            D("sync", srow[0:1, p * 1024:p * 1024 + QC],
              stats_d[R_M1:R_M1 + 1, t0:t0 + QC], writes=[f"C_srow{p}"])
            D("sync", srow[0:1, p * 1024 + QC:p * 1024 + 2 * QC],
              stats_d[R_R01:R_R01 + 1, t0:t0 + QC], writes=[f"C_srow{p}"])

        c_load(0)
        c_load(1)
        for ci in range(NCH):
            t0 = ci * QC
            p = ci % 2
            bcast_row(PS[3][:, 0:QC], srow[0:1, p * 1024:p * 1024 + QC],
                      writes=["ps3"], reads=[f"C_srow{p}"])
            bcast_row(PS[4][:, 0:QC], srow[0:1, p * 1024 + QC:p * 1024 + 2 * QC],
                      writes=["ps4"], reads=[f"C_srow{p}"])
            for ct in range(CT):
                tt(xch[:, ct, 1:QC + 1], xmb[:, p, ct, :], PS[3][:, 0:QC], ALU.subtract,
                   reads=[f"C_xm{p}", "ps3"], writes=["C_xc"])
                tt(xch[:, ct, 1:QC + 1], xch[:, ct, 1:QC + 1], PS[4][:, 0:QC], ALU.mult,
                   reads=["C_xc", "ps4"], writes=["C_xc"])
                ts(xch[:, ct, 1:QC + 1], xch[:, ct, 1:QC + 1], pcol("g1", ct), pcol("b1", ct),
                   ALU.mult, ALU.add, reads=["C_xc", "pcv"], writes=["C_xc"])
                tt(dt_[:, ct, :], xch[:, ct, 1:QC + 1], xch[:, ct, 0:QC], ALU.subtract,
                   reads=["C_xc"], writes=["C_d"])
                stt(mk[:, p, ct, :], dt_[:, ct, :], pcol("cm_k", ct), xch[:, ct, 0:QC],
                    ALU.mult, ALU.add, reads=["C_d", "C_xc", "pcv"], writes=[f"C_mk{p}"])
                stt(mr[:, p, ct, :], dt_[:, ct, :], pcol("cm_r", ct), xch[:, ct, 0:QC],
                    ALU.mult, ALU.add, reads=["C_d", "C_xc", "pcv"], writes=[f"C_mr{p}"])
            vcopy(xch[:, :, 0:1], xch[:, :, QC:QC + 1], reads=["C_xc"], writes=["C_xc"])
            if ci + 2 < NCH:
                c_load(ci + 2)
            for ft in range(FT):
                q = ft % 2
                pb = PS[q]
                for ct in range(CT):
                    mm(pb[:, 0:QC], wCK[:, ct, ft * 128:(ft + 1) * 128], mk[:, p, ct, :],
                       ct == 0, ct == CT - 1, reads=["C_wck", f"C_mk{p}"],
                       writes=[f"psk{q}"])
                act(rl[:, q, :], pb[:, 0:QC], AF.Relu, reads=[f"psk{q}"],
                    writes=[f"C_rl{q}"])
                tt(rq[:, q, :], rl[:, q, :], rl[:, q, :], ALU.mult, reads=[f"C_rl{q}"],
                   writes=[f"C_rq{q}"])
                D("gpsimd", rsq_v[:, ft, t0:t0 + QC], rq[:, q, :], reads=[f"C_rq{q}"])
            for ot in range(CT):
                q = ot % 2
                pb = PS[2] if q == 0 else PS[5]
                for ct in range(CT):
                    mm(pb[:, 0:QC], wCR[:, ct, ot * 128:(ot + 1) * 128], mr[:, p, ct, :],
                       ct == 0, ct == CT - 1, reads=["C_wcr", f"C_mr{p}"],
                       writes=[f"psr{q}"])
                act(rrt[:, p, ot, :], pb[:, 0:QC], AF.Sigmoid, reads=[f"psr{q}"],
                    writes=[f"C_rr{p}"])
            D("gpsimd", rr_v[:, :, t0:t0 + QC], rrt[:, p, :, :], reads=[f"C_rr{p}"])
        sch.set_barrier()
        ar.free()

    if "C2" in _PHASES:
        ar = nc.auto_arena()
        wCV = ar.sbuf_tensor("C2_wcv", [128, FT, C], FP16)
        stg = ar.sbuf_tensor("C2_stg", [128, C], F32)
        rsq = ar.sbuf_tensor("C2_rsq", [128, 2, FT, QC], FP16)
        xmb = ar.sbuf_tensor("C2_xm", [128, 2, CT, QC], FP16)
        rrt = ar.sbuf_tensor("C2_rr", [128, 2, CT, QC], FP16)
        st = ar.sbuf_tensor("C2_s", [128, 2, CT, QC], FP16)
        tmp = ar.sbuf_tensor("C2_t", [128, 2, QC], F32)
        ar.allocate()
        load_w_bf16(wCV, "C2_wcv", dview(wcv_d), FT, C, stg, "C2_stg")
        NCH = T // QC

        def c2_load(ci):
            p = ci % 2
            t0 = ci * QC
            D("sync", rsq[:, p, :, :], rsq_v[:, :, t0:t0 + QC], writes=[f"C2_rsq{p}"])
            D("sync", xmb[:, p, :, :], xmid_v[:, :, t0:t0 + QC], writes=[f"C2_xm{p}"])
            D("sync", rrt[:, p, :, :], rr_v[:, :, t0:t0 + QC], writes=[f"C2_rr{p}"])

        c2_load(0)
        c2_load(1)
        for ci in range(NCH):
            t0 = ci * QC
            p = ci % 2
            for ot in range(CT):
                q = ot % 2
                pb = PS[q]
                for ft in range(FT):
                    mm(pb[:, 0:QC], wCV[:, ft, ot * 128:(ot + 1) * 128], rsq[:, p, ft, :],
                       ft == 0, ft == FT - 1, reads=["C2_wcv", f"C2_rsq{p}"],
                       writes=[f"psc{q}"])
                tt(tmp[:, q, :], rrt[:, p, ot, :], pb[:, 0:QC], ALU.mult,
                   reads=[f"C2_rr{p}", f"psc{q}"], writes=[f"C2_t{q}"])
                tt(st[:, p, ot, :], xmb[:, p, ot, :], tmp[:, q, :], ALU.add,
                   reads=[f"C2_xm{p}", f"C2_t{q}"], writes=[f"C2_s{p}"])
            D("gpsimd", s_v[:, :, t0:t0 + QC], st[:, p, :, :], reads=[f"C2_s{p}"])
            if ci + 2 < NCH:
                c2_load(ci + 2)
        sch.set_barrier()
        ar.free()

    if "D" in _PHASES:
        ar = nc.auto_arena()
        wgT = ar.sbuf_tensor("D_wgT", [128, CT, II], FP16)
        wuT = ar.sbuf_tensor("D_wuT", [128, CT, II], FP16)
        stg = ar.sbuf_tensor("D_stg", [128, 2, C], F32)
        i32a = ar.sbuf_tensor("D_i32a", [128, C], I32)
        i32b = ar.sbuf_tensor("D_i32b", [128, C], I32)
        tma = ar.sbuf_tensor("D_tma", [128, C], F32)
        tmb = ar.sbuf_tensor("D_tmb", [128, C], F32)
        qw = ar.sbuf_tensor("D_qw", [128, 2, C], FP16)
        sg_s = ar.sbuf_tensor("D_sg", [128, IT], F32)
        su_s = ar.sbuf_tensor("D_su", [128, IT], F32)
        inv_s = ar.sbuf_tensor("D_inv", [128, 1], F32)
        osg = ar.sbuf_tensor("D_osg", [128, IT], F32)
        osu = ar.sbuf_tensor("D_osu", [128, IT], F32)
        xnb = ar.sbuf_tensor("D_xn", [128, 2, CT, QD], FP16)
        xq = ar.sbuf_tensor("D_xq", [128, 2, CT, QD], FP16)
        qa = ar.sbuf_tensor("D_qa", [128, 2, QD], I32)
        qb = ar.sbuf_tensor("D_qb", [128, 2, QD], I32)
        qta = ar.sbuf_tensor("D_qta", [128, 2, QD], F32)
        qtb = ar.sbuf_tensor("D_qtb", [128, 2, QD], F32)
        sgt = ar.sbuf_tensor("D_sgt", [128, 2, QD], FP16)
        upt = ar.sbuf_tensor("D_upt", [128, 2, QD], FP16)
        gut = ar.sbuf_tensor("D_gut", [128, 2, QD], FP16)
        amx = ar.sbuf_tensor("D_amx", [128, 1], F32)
        ar.allocate()

        for (w_dram, scol, skey, wT, wTkey) in (
                (wg_d, sg_s, "D_sg", wgT, "D_wgT"),
                (wu_d, su_s, "D_su", wuT, "D_wuT")):
            wv22 = dview(w_dram)
            for it in range(IT if _D_NIT is None else _D_NIT):
                p = it % 2
                D("gpsimd" if p else "sync", stg[:, p, :], wv22[:, it, :],
                  writes=[f"D_stg{p}"])
                redmax(scol[:, it:it + 1], stg[:, p, :], reads=[f"D_stg{p}"],
                       writes=[skey])
                ts(inv_s[:, 0:1], scol[:, it:it + 1], QMAXR, None, ALU.mult, ALU.bypass,
                   reads=[skey], writes=["D_inv"])
                act(inv_s[:, 0:1], inv_s[:, 0:1], AF.Ln, reads=["D_inv"], writes=["D_inv"])
                act(inv_s[:, 0:1], inv_s[:, 0:1], AF.Exp, scale=-1.0,
                    reads=["D_inv"], writes=["D_inv"])
                quant(qw[:, p, :], stg[:, p, :], inv_s[:, 0:1], i32a[:, 0:C], i32b[:, 0:C],
                      tma[:, 0:C], tmb[:, 0:C], [f"D_stg{p}", "D_inv"], f"D_qw{p}", "D_q")
                for kt in range(CT):
                    transpose(psT[:, 0:128], qw[:, p, kt * 128:(kt + 1) * 128],
                              reads=[f"D_qw{p}"], writes=["psTq"])
                    vcopy(wT[:, kt, it * 128:(it + 1) * 128], psT[:, 0:128],
                          reads=["psTq"], writes=[wTkey], eng="scalar")
            ts(scol[:, 0:IT], scol[:, 0:IT], QMAXR, None, ALU.mult, ALU.bypass,
               reads=[skey], writes=[skey])
        ts(osg[:, 0:IT], sg_s[:, 0:IT], sc1[:, 0:1], None, ALU.mult, ALU.bypass,
           reads=["D_sg", "sxcol"], writes=["D_osg"])
        ts(osu[:, 0:IT], su_s[:, 0:IT], sc1[:, 0:1], None, ALU.mult, ALU.bypass,
           reads=["D_su", "sxcol"], writes=["D_osu"])

        NCH = T // QD

        def d_load(ci):
            p = ci % 2
            D("sync", xnb[:, p, :, :], xn_v[:, :, ci * QD:(ci + 1) * QD],
              writes=[f"D_xn{p}"])

        def d_quant(ci):
            p = ci % 2
            for ct in range(CT):
                q = ct % 2
                quant(xq[:, p, ct, :], xnb[:, p, ct, :], sc1[:, 1:2],
                      qa[:, q, :], qb[:, q, :], qta[:, q, :], qtb[:, q, :],
                      [f"D_xn{p}", "sxcol"], f"D_xq{p}", f"D_x{q}")

        d_load(0)
        d_quant(0)
        d_load(1)
        for ci in range(0 if _D_TRUNC == 1 else NCH):
            t0 = ci * QD
            p = ci % 2
            if ci + 1 < NCH:
                d_quant(ci + 1)
            if ci + 2 < NCH:
                d_load(ci + 2)
            for it in range(IT):
                q = it % 2
                pg, pu = 2 * q, 2 * q + 1
                for ct in range(CT):
                    mm(PS[pg][:, 0:QD], wgT[:, ct, it * 128:(it + 1) * 128],
                       xq[:, p, ct, :], ct == 0, ct == CT - 1,
                       reads=["D_wgT", f"D_xq{p}"], writes=[f"ps{pg}"])
                for ct in range(CT):
                    mm(PS[pu][:, 0:QD], wuT[:, ct, it * 128:(it + 1) * 128],
                       xq[:, p, ct, :], ct == 0, ct == CT - 1,
                       reads=["D_wuT", f"D_xq{p}"], writes=[f"ps{pu}"])
                act(sgt[:, q, :], PS[pg][:, 0:QD], AF.Silu, scale=osg[:, it:it + 1],
                    reads=[f"ps{pg}", "D_osg"], writes=[f"D_sgt{q}"])
                act(upt[:, q, :], PS[pu][:, 0:QD], AF.Identity, scale=osu[:, it:it + 1],
                    reads=[f"ps{pu}", "D_osu"], writes=[f"D_upt{q}"])
                tt(gut[:, q, :], sgt[:, q, :], upt[:, q, :], ALU.mult,
                   reads=[f"D_sgt{q}", f"D_upt{q}"], writes=[f"D_gut{q}"])
                redmax(amx[:, 0:1], gut[:, q, :], reads=[f"D_gut{q}"], writes=["D_amx"])
                if ci == 0 and it == 0:
                    vcopy(amaxa[:, 0:1], amx[:, 0:1], reads=["D_amx"], writes=["amaxa"])
                else:
                    tt(amaxa[:, 0:1], amaxa[:, 0:1], amx[:, 0:1], ALU.max,
                       reads=["amaxa", "D_amx"], writes=["amaxa"])
                D("gpsimd", gu_v[:, it, t0:t0 + QD], gut[:, q, :], reads=[f"D_gut{q}"])
        if _D_TRUNC == 0:
            scale_pipeline(cc_in2, cc_out2, sc2, 1, "sdcol")
        sch.set_barrier()
        ar.free()

    if "E" in _PHASES:
        ar = nc.auto_arena()
        wdT = ar.sbuf_tensor("E_wdT", [128, IT, C], FP16)
        stg = ar.sbuf_tensor("E_stg", [128, II], F32)
        i32a = ar.sbuf_tensor("E_i32a", [128, 704], I32)
        i32b = ar.sbuf_tensor("E_i32b", [128, 704], I32)
        tma = ar.sbuf_tensor("E_tma", [128, 704], F32)
        tmb = ar.sbuf_tensor("E_tmb", [128, 704], F32)
        qw = ar.sbuf_tensor("E_qw", [128, II], FP16)
        sd_s = ar.sbuf_tensor("E_sd", [128, CT], F32)
        inv_s = ar.sbuf_tensor("E_inv", [128, 1], F32)
        osd = ar.sbuf_tensor("E_osd", [128, CT], F32)
        gub = ar.sbuf_tensor("E_gu", [128, 2, IT, QE], FP16)
        qa = ar.sbuf_tensor("E_qa", [128, QE], I32)
        qb = ar.sbuf_tensor("E_qb", [128, QE], I32)
        qta = ar.sbuf_tensor("E_qta", [128, QE], F32)
        qtb = ar.sbuf_tensor("E_qtb", [128, QE], F32)
        xch = ar.sbuf_tensor("E_x", [128, CT, QE], F32)
        sb = ar.sbuf_tensor("E_s", [128, 2, CT, QE], FP16)
        gt = ar.sbuf_tensor("E_g", [128, 2, QE], FP16)
        tb = ar.sbuf_tensor("E_t", [128, 2, QE], FP16)
        x1t = ar.sbuf_tensor("E_x1", [128, 2, QE], F32)
        xb16 = ar.sbuf_tensor("E_xb", [128, CT, QE], FP16)
        sq16 = ar.sbuf_tensor("E_sq", [128, CT, QE], FP16)
        ar.allocate()

        wv8 = dview(wd_d)
        for ht in range(CT):
            p = ht % 2
            D("gpsimd" if p else "sync", stg[:, :], wv8[:, ht, :],
              writes=["E_stg"])
            redmax(sd_s[:, ht:ht + 1], stg[:, :], reads=["E_stg"], writes=["E_sd"])
            ts(inv_s[:, 0:1], sd_s[:, ht:ht + 1], QMAXR, None, ALU.mult, ALU.bypass,
               reads=["E_sd"], writes=["E_inv"])
            act(inv_s[:, 0:1], inv_s[:, 0:1], AF.Ln, reads=["E_inv"], writes=["E_inv"])
            act(inv_s[:, 0:1], inv_s[:, 0:1], AF.Exp, scale=-1.0,
                reads=["E_inv"], writes=["E_inv"])
            for sg0 in range(0, II, 704):
                quant(qw[:, sg0:sg0 + 704], stg[:, sg0:sg0 + 704], inv_s[:, 0:1],
                      i32a[:, :], i32b[:, :], tma[:, :], tmb[:, :],
                      ["E_stg", "E_inv"], "E_qw", "E_q")
            for it in range(IT):
                transpose(psT[:, 0:128], qw[:, it * 128:(it + 1) * 128],
                          reads=["E_qw"], writes=["psTq"])
                vcopy(wdT[:, it, ht * 128:(ht + 1) * 128], psT[:, 0:128],
                      reads=["psTq"], writes=["E_wdT"], eng="scalar")
        ts(sd_s[:, 0:CT], sd_s[:, 0:CT], QMAXR, None, ALU.mult, ALU.bypass,
           reads=["E_sd"], writes=["E_sd"])
        ts(osd[:, 0:CT], sd_s[:, 0:CT], sc2[:, 0:1], None, ALU.mult, ALU.bypass,
           reads=["E_sd", "sdcol"], writes=["E_osd"])

        NCH = T // QE

        def e_load(ci):
            p = ci % 2
            t0 = ci * QE
            D("sync", gub[:, p, :, :], gu_v[:, :, t0:t0 + QE], writes=[f"E_gu{p}"])
            D("sync", sb[:, p, :, :], s_v[:, :, t0:t0 + QE], writes=[f"E_s{p}"])

        def e_quant(ci):
            p = ci % 2
            for it in range(IT):
                quant(gub[:, p, it, :], gub[:, p, it, :], sc2[:, 1:2],
                      qa[:, :], qb[:, :], qta[:, :], qtb[:, :],
                      [f"E_gu{p}", "sdcol"], f"E_gu{p}", "E_q2")

        e_load(0)
        e_quant(0)
        e_load(1)
        for ci in range(NCH):
            t0 = ci * QE
            p = ci % 2
            D("sync", xch[:, :, :], x_v[:, :, t0:t0 + QE], writes=["E_x"])
            if ci + 1 < NCH:
                e_quant(ci + 1)
            for ht in range(CT):
                q = ht % 2
                pb = PS[q]
                for it in range(IT):
                    mm(pb[:, 0:QE], wdT[:, it, ht * 128:(ht + 1) * 128],
                       gub[:, p, it, :], it == 0, it == IT - 1,
                       reads=["E_wdT", f"E_gu{p}"], writes=[f"pse{q}"])
                act(gt[:, q, :], pb[:, 0:QE], AF.Silu, scale=osd[:, ht:ht + 1],
                    reads=[f"pse{q}", "E_osd"], writes=[f"E_g{q}"])
                tt(tb[:, q, :], sb[:, p, ht, :], gt[:, q, :], ALU.mult,
                   reads=[f"E_s{p}", f"E_g{q}"], writes=[f"E_t{q}"])
                tt(x1t[:, q, :], xch[:, ht, :], tb[:, q, :], ALU.add,
                   reads=["E_x", f"E_t{q}"], writes=[f"E_x1{q}"])
                act(xb16[:, ht, :], x1t[:, q, :], AF.Identity, reads=[f"E_x1{q}"],
                    writes=["E_xb"])
                act(sq16[:, ht, :], x1t[:, q, :], AF.Square, reads=[f"E_x1{q}"],
                    writes=["E_sq"])
            stats_rows(xb16, sq16, QE, t0, "E_xb", "E_sq",
                       [(R_MF, "m"), (R_R0F, "r0"), (R_RFC, "rfc")])
            D("gpsimd", x1_v[:, :, t0:t0 + QE], xb16[:, :, :], reads=["E_xb"])
            if ci + 2 < NCH:
                e_load(ci + 2)
        sch.set_barrier()
        ar.free()

    if "F1" in _PHASES:
        ar = nc.auto_arena()
        wIN = [ar.sbuf_tensor(f"F_win{l}", [128, 4, 2 * DI], FP16) for l in range(2)]
        wXP = [ar.sbuf_tensor(f"F_wxp{l}", [128, CT, 64], FP16) for l in range(2)]
        wDT = [ar.sbuf_tensor(f"F_wdt{l}", [32, DI], FP16) for l in range(2)]
        wMO = [ar.sbuf_tensor(f"F_wmo{l}", [128, CT, DC], FP16) for l in range(2)]
        wPO = ar.sbuf_tensor("F_wpo", [128, CT, C], FP16)
        stg = ar.sbuf_tensor("F_stg", [128, 512], F32)
        sL16 = ar.sbuf_tensor("F_sL16", [128, CT, QF1], FP16)
        sL = ar.sbuf_tensor("F_sL", [128, CT, QF1], FP16)
        hh = ar.sbuf_tensor("F_h", [128, 4, QF1], F32)
        hn = ar.sbuf_tensor("F_hn", [128, 4, QF1], FP16)
        hsq = ar.sbuf_tensor("F_hsq", [128, 4, QF1], FP16)
        xc = [ar.sbuf_tensor(f"F_xc{l}", [128, CT, QF1 + 3], FP16) for l in range(2)]
        zs = ar.sbuf_tensor("F_z", [128, CT, QF1], FP16)
        cvt = ar.sbuf_tensor("F_cv", [128, QF1], F32)
        xcs = ar.sbuf_tensor("F_xcs", [128, CT, QF1], FP16)
        dbc = ar.sbuf_tensor("F_dbc", [64, QF1], FP16)
        delt = ar.sbuf_tensor("F_del", [128, CT, QF1], FP16)
        ut = ar.sbuf_tensor("F_u", [128, CT, QF1], FP16)
        yac = ar.sbuf_tensor("F_yac", [128, CT, QF1], FP16)
        bmb = ar.sbuf_tensor("F_bmb", [128, 2, QF1], FP16)
        dA = ar.sbuf_tensor("F_dA", [128, 2, QF1], FP16)
        dBx = ar.sbuf_tensor("F_dBx", [128, 2, QF1], FP16)
        ht = ar.sbuf_tensor("F_ht", [128, 2, QF1], FP16)
        carr = [ar.sbuf_tensor(f"F_carr{l}", [128, 128], F32) for l in range(2)]
        y3 = ar.sbuf_tensor("F_y3", [128, CT, QF1], FP16)
        sel = ar.sbuf_tensor("sel_t", [64, 32 * 128], FP16)
        ar.allocate()
        D("sync", sel[:, :], sel_d, writes=["sel"])
        catb = y3
        s1t = zs

        for l in range(2):
            load_w_bf16(wIN[l], f"F_win{l}", dview(inw_d[l]), 4, 2 * DI, stg, "F_stg",
                        seg=512)
            load_w_bf16(wXP[l], f"F_wxp{l}", dview(xpw_d[l]), CT, 64, stg, "F_stg",
                        seg=512)
            load_w_bf16(wMO[l], f"F_wmo{l}", dview(mow_d[l]), CT, DC, stg, "F_stg",
                        seg=512)
            for s0 in range(0, DI, 512):
                D("gpsimd", stg[0:32, 0:512], dtw_d[l][:, s0:s0 + 512], writes=["F_stg"])
                vcopy(wDT[l][:, s0:s0 + 512], stg[0:32, 0:512], reads=["F_stg"],
                      writes=[f"F_wdt{l}"], eng="gpsimd")
            memset(xc[l][:, :, 0:3], 0.0, writes=[f"F_xc{l}"])
            memset(carr[l][:, :], 0.0, writes=[f"F_carr{l}"])
        load_w_bf16(wPO, "F_wpo", dview(pow_d), CT, C, stg, "F_stg", seg=512)

        for ci in range(T // QF1):
            t0 = ci * QF1
            D("sync", sL16[:, :, :], x1_v[:, :, t0:t0 + QF1], writes=["F_sL16"])
            D("sync", srow[0:1, 0:QF1], stats_d[R_MF:R_MF + 1, t0:t0 + QF1], writes=["srow"])
            D("sync", srow[0:1, QF1:2 * QF1], stats_d[R_RFC:R_RFC + 1, t0:t0 + QF1],
              writes=["srow"])
            bcast_row(PS[4][:, 0:QF1], srow[0:1, 0:QF1], writes=["ps4"])
            bcast_row(PS[5][:, 0:QF1], srow[0:1, QF1:2 * QF1], writes=["ps5"])
            for ct in range(CT):
                tt(sL[:, ct, :], sL16[:, ct, :], PS[4][:, 0:QF1], ALU.subtract,
                   reads=["F_sL16", "ps4"], writes=["F_sL"])
                tt(sL[:, ct, :], sL[:, ct, :], PS[5][:, 0:QF1], ALU.mult,
                   reads=["F_sL", "ps5"], writes=["F_sL"])
                ts(sL[:, ct, :], sL[:, ct, :], pcol("gL", ct), pcol("bL", ct),
                   ALU.mult, ALU.add, reads=["F_sL", "pcv"], writes=["F_sL"])
            for j in range(4):
                vcopy(hh[:, j, :], sL[:, j, :], reads=["F_sL"], writes=["F_h"])
            for l in range(2):
                lk = str(l)
                for j in range(4):
                    act(hsq[:, j, :], hh[:, j, :], AF.Square, reads=["F_h"], writes=["F_hsq"])
                    act(hn[:, j, :], hh[:, j, :], AF.Identity, reads=["F_h"], writes=["F_hnb"])
                for j in range(4):
                    mm(pstat[0:1, 0:QF1], onesb[:, :], hsq[:, j, :], j == 0, j == 3,
                       reads=["F_hsq", "onesb"], writes=["pstat"])
                smul(srow[0:1, 0:QF1], pstat[0:1, 0:QF1], 1.0 / DC,
                     reads=["pstat"], writes=["srow"])
                act(srow[0:1, 0:QF1], srow[0:1, 0:QF1], AF.Ln, bias=cbias[0:1, 0:1],
                    reads=["srow", "cbias"], writes=["srow"])
                act(srow[0:1, 0:QF1], srow[0:1, 0:QF1], AF.Exp, scale=-0.5,
                    reads=["srow"], writes=["srow"])
                bcast_row(PS[4][:, 0:QF1], srow[0:1, 0:QF1], writes=["ps4"])
                for j in range(4):
                    tt(hn[:, j, :], hn[:, j, :], PS[4][:, 0:QF1], ALU.mult,
                       reads=["F_hnb", "ps4"], writes=["F_hnb"])
                    ts(hn[:, j, :], hn[:, j, :],
                       pcv[:, PC["rms" + lk] * 8 + j: PC["rms" + lk] * 8 + j + 1], None,
                       ALU.mult, ALU.bypass, reads=["F_hnb", "pcv"], writes=["F_hnb"])
                for ot in range(16):
                    pb = PS[ot % 2]
                    for j in range(4):
                        mm(pb[:, 0:QF1], wIN[l][:, j, ot * 128:(ot + 1) * 128], hn[:, j, :],
                           j == 0, j == 3, reads=[f"F_win{l}", "F_hnb"],
                           writes=[f"psf{ot % 2}"])
                    if ot < 8:
                        vcopy(xc[l][:, ot, 3:QF1 + 3], pb[:, 0:QF1],
                              reads=[f"psf{ot % 2}"], writes=[f"F_xc{l}"])
                    else:
                        act(zs[:, ot - 8, :], pb[:, 0:QF1], AF.Silu,
                            reads=[f"psf{ot % 2}"], writes=["F_zs"])
                for ct in range(CT):
                    cw = lambda j, ct=ct: convw[:, l * 32 + ct * 4 + j: l * 32 + ct * 4 + j + 1]
                    ts(cvt[:, :], xc[l][:, ct, 0:QF1], cw(0), None, ALU.mult, ALU.bypass,
                       reads=[f"F_xc{l}", "convw"], writes=["F_cv"])
                    for j in (1, 2, 3):
                        stt(cvt[:, :], xc[l][:, ct, j:QF1 + j], cw(j), cvt[:, :],
                            ALU.mult, ALU.add, reads=[f"F_xc{l}", "convw", "F_cv"],
                            writes=["F_cv"])
                    act(xcs[:, ct, :], cvt[:, :], AF.Silu,
                        bias=pcv[:, PC["cb" + lk] * 8 + ct: PC["cb" + lk] * 8 + ct + 1],
                        reads=["F_cv", "pcv"], writes=["F_xcs"])
                    vcopy(xc[l][:, ct, 0:3], xc[l][:, ct, QF1:QF1 + 3],
                          reads=[f"F_xc{l}"], writes=[f"F_xc{l}"])
                for ct in range(CT):
                    mm(PS[2][0:64, 0:QF1], wXP[l][:, ct, :], xcs[:, ct, :],
                       ct == 0, ct == CT - 1, reads=[f"F_wxp{l}", "F_xcs"], writes=["ps2"])
                act(dbc[:, :], PS[2][0:64, 0:QF1], AF.Identity, reads=["ps2"], writes=["F_dbc"])
                for ot in range(CT):
                    mm(PS[3][:, 0:QF1], wDT[l][:, ot * 128:(ot + 1) * 128], dbc[0:32, :],
                       True, True, reads=[f"F_wdt{l}", "F_dbc"], writes=["ps3"])
                    act(cvt[:, :], PS[3][:, 0:QF1], AF.Exp,
                        bias=pcv[:, PC["dtb" + lk] * 8 + ot: PC["dtb" + lk] * 8 + ot + 1],
                        reads=["ps3", "pcv"], writes=["F_cv"])
                    act(delt[:, ot, :], cvt[:, :], AF.Ln, bias=cbias[:, 3:4],
                        reads=["F_cv", "cbias"], writes=["F_del"])
                    tt(ut[:, ot, :], delt[:, ot, :], xcs[:, ot, :], ALU.mult,
                       reads=["F_del", "F_xcs"], writes=["F_u"])
                for n in range(NS):
                    pp = n % 2
                    mm(PS[4][:, 0:QF1], sel[:, (2 * n) * 128:(2 * n + 1) * 128], dbc[:, :],
                       True, True, reads=["sel", "F_dbc"], writes=["ps4"])
                    vcopy(bmb[:, 0, :], PS[4][:, 0:QF1], reads=["ps4"], writes=["F_bm"],
                          eng="scalar")
                    mm(PS[5][:, 0:QF1], sel[:, (2 * n + 1) * 128:(2 * n + 2) * 128],
                       dbc[:, :], True, True, reads=["sel", "F_dbc"], writes=["ps5"])
                    vcopy(bmb[:, 1, :], PS[5][:, 0:QF1], reads=["ps5"], writes=["F_cm"],
                          eng="scalar")
                    for ct in range(CT):
                        idx = ct * NS + n
                        act(dA[:, pp, :], delt[:, ct, :], AF.Exp,
                            scale=aneg[:, l * 128 + idx: l * 128 + idx + 1],
                            reads=["F_del", "aneg"], writes=[f"F_dA{pp}"])
                        tt(dBx[:, pp, :], ut[:, ct, :], bmb[:, 0, :], ALU.mult,
                           reads=["F_u", "F_bm"], writes=[f"F_dBx{pp}"])
                        scan(ht[:, pp, :], dA[:, pp, :], dBx[:, pp, :],
                             carr[l][:, idx:idx + 1],
                             reads=[f"F_dA{pp}", f"F_dBx{pp}", f"F_carr{l}"],
                             writes=[f"F_ht{pp}"])
                        vcopy(carr[l][:, idx:idx + 1], ht[:, pp, QF1 - 1:QF1],
                              reads=[f"F_ht{pp}"], writes=[f"F_carr{l}"])
                        tt(dBx[:, pp, :], ht[:, pp, :], bmb[:, 1, :], ALU.mult,
                           reads=[f"F_ht{pp}", "F_cm"], writes=[f"F_dBx{pp}"])
                        if n == 0:
                            vcopy(yac[:, ct, :], dBx[:, pp, :], reads=[f"F_dBx{pp}"],
                                  writes=["F_yac"])
                        else:
                            tt(yac[:, ct, :], yac[:, ct, :], dBx[:, pp, :], ALU.add,
                               reads=["F_yac", f"F_dBx{pp}"], writes=["F_yac"])
                for ct in range(CT):
                    stt(cvt[:, :], xcs[:, ct, :],
                        pcv[:, PC["Dp" + lk] * 8 + ct: PC["Dp" + lk] * 8 + ct + 1],
                        yac[:, ct, :], ALU.mult, ALU.add,
                        reads=["F_xcs", "pcv", "F_yac"], writes=["F_cv"])
                    tt(y3[:, ct, :], cvt[:, :], zs[:, ct, :], ALU.mult,
                       reads=["F_cv", "F_zs"], writes=["F_y3"])
                for ot in range(4):
                    pb = PS[ot % 2]
                    for ct in range(CT):
                        mm(pb[:, 0:QF1], wMO[l][:, ct, ot * 128:(ot + 1) * 128],
                           y3[:, ct, :], ct == 0, ct == CT - 1,
                           reads=[f"F_wmo{l}", "F_y3"], writes=[f"psf{ot % 2}"])
                    tt(hh[:, ot, :], hh[:, ot, :], pb[:, 0:QF1], ALU.add,
                       reads=["F_h", f"psf{ot % 2}"], writes=["F_h"])
            for j in range(4):
                vcopy(catb[:, j, :], sL[:, j, :], reads=["F_sL"], writes=["F_y3"])
                tt(catb[:, 4 + j, :], sL[:, 4 + j, :], hh[:, j, :], ALU.add,
                   reads=["F_sL", "F_h"], writes=["F_y3"])
            for ot in range(CT):
                pb = PS[ot % 2]
                for ct in range(CT):
                    mm(pb[:, 0:QF1], wPO[:, ct, ot * 128:(ot + 1) * 128], catb[:, ct, :],
                       ct == 0, ct == CT - 1, reads=["F_wpo", "F_y3"],
                       writes=[f"psf{ot % 2}"])
                stt(s1t[:, ot, :], pb[:, 0:QF1], pcol("po_b", ot), sL[:, ot, :],
                    ALU.add, ALU.add, reads=[f"psf{ot % 2}", "pcv", "F_sL"],
                    writes=["F_zs"])
            D("sync", s1_v[:, :, t0:t0 + QF1], s1t[:, :, :], reads=["F_zs"])
        sch.set_barrier()
        ar.free()

    if "F2" in _PHASES:
        ar = nc.auto_arena()
        wFC = ar.sbuf_tensor("G_wfc", [128, CT, FF], FP16)
        wPR = ar.sbuf_tensor("G_wpr", [128, FT, C], FP16)
        stg = ar.sbuf_tensor("G_stg", [128, 512], F32)
        rsqb = ar.sbuf_tensor("G_rsq", [128, 2, FT, QF2], FP16)
        x1b = ar.sbuf_tensor("G_x1", [128, 3, CT, QF2], FP16)
        xnb = ar.sbuf_tensor("G_xn", [128, 2, CT, QF2], FP16)
        rl = ar.sbuf_tensor("G_rl", [128, 3, QF2], FP16)
        tf = ar.sbuf_tensor("G_tf", [128, QF2], F32)
        accs = ar.sbuf_tensor("G_acc", [128, 2, QF2], FP16)
        g1 = ar.sbuf_tensor("G_g1", [128, QF2], FP16)
        s1b = ar.sbuf_tensor("G_s1", [128, 2, CT, QF2], FP16)
        tb = ar.sbuf_tensor("G_t", [128, QF2], FP16)
        ot_f = ar.sbuf_tensor("G_out", [128, QF2], F32)
        ar.allocate()
        load_w_bf16(wFC, "G_wfc", dview(wfc_d), CT, FF, stg, "G_stg", seg=512)
        load_w_bf16(wPR, "G_wpr", dview(wpr_d), FT, C, stg, "G_stg", seg=512)
        NCH = T // QF2

        def ybank(ot):
            return PS[2 + ot // 2][:, (ot % 2) * QF2:(ot % 2 + 1) * QF2]

        def g_load(ci):
            p = ci % 2
            p3 = ci % 3
            t0 = ci * QF2
            D("sync", x1b[:, p3, :, :], x1_v[:, :, t0:t0 + QF2], writes=[f"G_x1{p3}"])
            D("sync", srow[0:1, p * 512:p * 512 + QF2],
              stats_d[R_MF:R_MF + 1, t0:t0 + QF2], writes=[f"G_srow{p}"])
            D("sync", srow[0:1, p * 512 + QF2:p * 512 + 2 * QF2],
              stats_d[R_R0F:R_R0F + 1, t0:t0 + QF2], writes=[f"G_srow{p}"])

        def g_load_s1(ci):
            p = ci % 2
            t0 = ci * QF2
            D("sync", s1b[:, p, :, :], s1_v[:, :, t0:t0 + QF2], writes=[f"G_s1{p}"])

        def g_prep(ci):
            p = ci % 2
            p3 = ci % 3
            bcast_row(PS[0][:, 0:QF2], srow[0:1, p * 512:p * 512 + QF2],
                      writes=["ps0"], reads=[f"G_srow{p}"])
            bcast_row(PS[1][:, 0:QF2], srow[0:1, p * 512 + QF2:p * 512 + 2 * QF2],
                      writes=["ps1"], reads=[f"G_srow{p}"])
            for ct in range(CT):
                tt(tf[:, :], x1b[:, p3, ct, :], PS[0][:, 0:QF2],
                   ALU.subtract, reads=[f"G_x1{p3}", "ps0"], writes=["G_tf"])
                tt(xnb[:, p, ct, :], tf[:, :],
                   PS[1][:, 0:QF2], ALU.mult,
                   reads=["G_tf", "ps1"], writes=[f"G_xn{p}"])

        def g_fc(ci):
            p = ci % 2
            for ft in range(FT):
                q = ft % 3
                pb = PS[q] if q < 2 else pstat
                kq = f"ps{q}" if q < 2 else "ps6"
                for ct in range(CT):
                    mm(pb[:, 0:QF2], wFC[:, ct, ft * 128:(ft + 1) * 128],
                       xnb[:, p, ct, :], ct == 0, ct == CT - 1,
                       reads=["G_wfc", f"G_xn{p}"], writes=[kq])
                act(rl[:, q, :], pb[:, 0:QF2], AF.Relu, reads=[kq],
                    writes=[f"G_rl{q}"])
                tt(rsqb[:, p, ft, :], rl[:, q, :], rl[:, q, :], ALU.mult,
                   reads=[f"G_rl{q}"], writes=[f"G_rsq{p}"])

        def g_proj(ci):
            p = ci % 2
            p3 = ci % 3
            t0 = ci * QF2
            for ot in range(CT):
                q = ot % 2
                for ft in range(FT):
                    mm(ybank(ot), wPR[:, ft, ot * 128:(ot + 1) * 128],
                       rsqb[:, p, ft, :], ft == 0, ft == FT - 1,
                       reads=["G_wpr", f"G_rsq{p}"], writes=[f"psy{ot // 2}"])
                ts(accs[:, q, :], ybank(ot), THS[0], None, ALU.is_gt, ALU.bypass,
                   reads=[f"psy{ot // 2}"], writes=[f"G_acc{q}"])
                for th in THS[1:]:
                    stt(accs[:, q, :], ybank(ot), th, accs[:, q, :], ALU.is_gt, ALU.add,
                        reads=[f"psy{ot // 2}", f"G_acc{q}"], writes=[f"G_acc{q}"])
                act(g1[:, :], accs[:, q, :], AF.Silu, scale=0.2,
                    reads=[f"G_acc{q}"], writes=["G_g1"])
                tt(tb[:, :], s1b[:, p, ot, :], g1[:, :], ALU.mult,
                   reads=[f"G_s1{p}", "G_g1"], writes=["G_t"])
                tt(ot_f[:, :], x1b[:, p3, ot, :], tb[:, :], ALU.add,
                   reads=[f"G_x1{p3}", "G_t"], writes=["G_out"])
                D("gpsimd", out_v[:, ot, t0:t0 + QF2], ot_f[:, :],
                  reads=["G_out"])
                if _DBG_G:
                    D("gpsimd", rr_v[:, ot, t0:t0 + QF2], g1[:, :],
                      reads=["G_g1"])

        g_load(0)
        g_prep(0)
        g_load(1)
        g_prep(1)
        g_load_s1(0)
        for ci in range(NCH if _F2_TRUNC != 1 else 0):
            g_fc(ci)
            if _F2_TRUNC != 2 and ci > 0:
                g_proj(ci - 1)
            if ci + 1 < NCH:
                g_load_s1(ci + 1)
            if ci + 2 < NCH:
                g_load(ci + 2)
            if ci + 2 < NCH:
                g_prep(ci + 2)
        if _F2_TRUNC == 0:
            g_proj(NCH - 1)
        ar.free()

    sch.set_barrier()
    E("sync", lambda e: e.nop())

    sch.materialize()
    with nc.Block() as block:
        @block.sync
        def _(eng):
            sch.run_engine("sync", eng)

        @block.vector
        def _(eng):
            sch.run_engine("vector", eng)

        @block.scalar
        def _(eng):
            sch.run_engine("scalar", eng)

        @block.tensor
        def _(eng):
            sch.run_engine("tensor", eng)

        @block.gpsimd
        def _(eng):
            sch.run_engine("gpsimd", eng)
    sch.close()
    for c in reversed(psum_ctx):
        c.__exit__(None, None, None)
    per.free()
    return nc


def _pack_pcvec(inp):
    pc = np.zeros((128, 26 * 8), dtype=np.float32)
    def put(slot_idx, v):
        pc[:, slot_idx * 8:(slot_idx + 1) * 8] = np.asarray(v).reshape(8, 128).T
    put(0, inp["L0_attn_tm_k"]); put(1, inp["L0_attn_tm_v"]); put(2, inp["L0_attn_tm_r"])
    put(3, inp["L0_attn_time_decay"]); put(4, inp["L0_attn_time_first"])
    put(5, inp["L0_attn_ln0_g"]); put(6, inp["L0_attn_ln0_b"])
    put(7, inp["L0_attn_ln1_g"]); put(8, inp["L0_attn_ln1_b"])
    put(9, inp["L0_attn_cm_k"]); put(10, inp["L0_attn_cm_r"])
    put(11, inp["L1_attn_ln_g"]); put(12, inp["L1_attn_ln_b"])
    put(13, inp["L1_attn_po_b"])
    pc[:, 14 * 8:15 * 8] = np.asarray(inp["L1_attn_conv_b"][0]).reshape(8, 128).T
    pc[:, 15 * 8:16 * 8] = np.asarray(inp["L1_attn_conv_b"][1]).reshape(8, 128).T
    pc[:, 16 * 8:17 * 8] = np.asarray(inp["L1_attn_dt_b"][0]).reshape(8, 128).T
    pc[:, 17 * 8:18 * 8] = np.asarray(inp["L1_attn_dt_b"][1]).reshape(8, 128).T
    pc[:, 18 * 8:19 * 8] = np.asarray(inp["L1_attn_Dp"][0]).reshape(8, 128).T
    pc[:, 19 * 8:20 * 8] = np.asarray(inp["L1_attn_Dp"][1]).reshape(8, 128).T
    pc[:, 20 * 8:20 * 8 + 4] = np.asarray(inp["L1_attn_rms_w"][0]).reshape(4, 128).T
    pc[:, 21 * 8:21 * 8 + 4] = np.asarray(inp["L1_attn_rms_w"][1]).reshape(4, 128).T
    return pc


def kernel(**inp):
    nc = _build()
    cc = np.ascontiguousarray

    pcvec = _pack_pcvec(inp)
    alog = cc(np.asarray(inp["L1_attn_A_log"]).reshape(2, 8, 128, 16)
              .transpose(2, 0, 1, 3).reshape(128, 256)).astype(np.float32)
    convw = cc(np.asarray(inp["L1_attn_conv_w"]).reshape(2, 8, 128, 4)
               .transpose(2, 0, 1, 3).reshape(128, 64)).astype(np.float32)
    clips = np.array([np.asarray(inp["L0_ffn_clip_x"])[0],
                      np.asarray(inp["L0_ffn_clip_down"])[0]], dtype=np.float32)
    ident = np.eye(128, dtype=np.float16)
    sel = np.zeros((64, 32 * 128), dtype=np.float16)
    for n in range(16):
        sel[32 + n, (2 * n) * 128:(2 * n + 1) * 128] = 1
        sel[48 + n, (2 * n + 1) * 128:(2 * n + 2) * 128] = 1

    f32 = lambda a: np.asarray(a, dtype=np.float32)
    shared = dict(
        pcvec=pcvec, Alog=alog, convw=convw, clips=clips, ident=ident, sel=sel,
        Wk=cc(f32(inp["L0_attn_Wk"])), Wv=cc(f32(inp["L0_attn_Wv"])),
        Wr=cc(f32(inp["L0_attn_Wr"])), Wo=cc(f32(inp["L0_attn_Wo"])),
        Wck=cc(f32(inp["L0_attn_Wck"])), Wcv=cc(f32(inp["L0_attn_Wcv"])),
        Wcr=cc(f32(inp["L0_attn_Wcr"])),
        wg_n=cc(f32(inp["L0_ffn_w_gate"])), wu_n=cc(f32(inp["L0_ffn_w_up"])),
        wd_n=cc(f32(inp["L0_ffn_w_down"])),
        inwT=cc(f32(inp["L1_attn_in_W"]).transpose(0, 2, 1)),
        xpwT=cc(f32(inp["L1_attn_xp_W"]).transpose(0, 2, 1)),
        dtwT=cc(f32(inp["L1_attn_dt_W"]).transpose(0, 2, 1)),
        mowT=cc(f32(inp["L1_attn_mo_W"]).transpose(0, 2, 1)),
        powT=cc(f32(inp["L1_attn_po_W"]).T),
        wfcT=cc(f32(inp["L1_ffn_W_fc"]).T),
        wprT=cc(f32(inp["L1_ffn_W_proj"]).T),
    )
    x = f32(inp["L0_attn_x"])
    in_maps = []
    for i in range(8):
        m = dict(shared)
        m["x"] = cc(x[i].T)
        in_maps.append(m)
    res = run_bass_kernel_spmd(nc, in_maps, core_ids=list(range(8)))
    outs = [np.asarray(r["out"]) for r in res.results]
    full = np.stack([o.T for o in outs], axis=0)
    if _DEBUG_OUT:
        kernel._dbg = [{k: np.asarray(v) for k, v in r.items() if k != "out"}
                       for r in res.results]
    return np.ascontiguousarray(full.astype(np.float32))


# version 07 (final)
# speedup vs baseline: 1.4426x; 1.4426x over previous
import numpy as np
import ml_dtypes
import concourse.bass as bass
import concourse.mybir as mybir
from concourse.bass_utils import run_bass_kernel_spmd

F32 = mybir.dt.float32
BF16 = mybir.dt.bfloat16
FP16 = mybir.dt.float16
I32 = mybir.dt.int32
AF = mybir.ActivationFunctionType
ALU = mybir.AluOpType
AX = mybir.AxisListType

C = 1024
T = 4096
CT = 8
FF = 4096
FT = 32
II = 2816
IT = 22
DC = 512
DI = 1024
NS = 16
EPS = 1e-5
QMAXR = 1.0 / 127.0

QA, QB, QC, QD, QE, QF1, QF2 = 512, 256, 512, 512, 512, 512, 256

THS = (0.24419428096993973, 0.4639691338428854, 0.6861859295255306,
       0.9059607823984763, 1.0839784132255623)

_PHASES = ("A", "B", "C1", "C2", "D", "E", "F1", "F2")
_DEBUG_OUT = ()
_D_TRUNC = 0
_D_NIT = None
_F2_TRUNC = 0
_DBG_G = False

ENGINES = ("tensor", "vector", "scalar", "gpsimd", "sync")


class Sched:
    NDSEM = 8

    def __init__(self, nc):
        self.nc = nc
        self.ops = {e: [] for e in ENGINES}
        self.last_writer = {}
        self.readers = {}
        self.barrier = []
        self.dma_ord = {e: 0 for e in ENGINES}
        self.last_dma = {}

    def emit(self, eng, fn, deps=(), reads=(), writes=(), is_dma=False):
        idx = len(self.ops[eng])
        tok = (eng, idx)
        alldeps = list(deps) + list(self.barrier)
        for k in reads:
            w = self.last_writer.get(k)
            if w is not None:
                alldeps.append(w)
        for k in writes:
            w = self.last_writer.get(k)
            if w is not None:
                alldeps.append(w)
            alldeps.extend(self.readers.get(k, ()))
        seen = set()
        dd = []
        for d in alldeps:
            if d != tok and d not in seen:
                seen.add(d)
                if eng == "tensor" and d[0] == "tensor":
                    continue
                dd.append(d)
        op = dict(fn=fn, deps=dd, is_dma=is_dma, inc=False)
        if is_dma:
            k = self.dma_ord[eng]
            self.dma_ord[eng] = k + 1
            op["dsem"] = k % self.NDSEM
            op["dcnt"] = k // self.NDSEM
            self.last_dma[(eng, op["dsem"])] = tok
        self.ops[eng].append(op)
        for k in reads:
            self.readers.setdefault(k, []).append(tok)
        for k in writes:
            self.last_writer[k] = tok
            self.readers[k] = []
        return tok

    def dma(self, eng, out, in_, reads=(), writes=(), deps=()):
        return self.emit(eng, (lambda e, o=out, i=in_: e.dma_start(out=o, in_=i)),
                         deps=deps, reads=reads, writes=writes, is_dma=True)

    def barrier_like(self, keys):
        toks = []
        for k in keys:
            w = self.last_writer.get(k)
            if w is not None:
                toks.append(w)
        return toks

    def set_barrier(self):
        toks = []
        for e in ENGINES:
            if self.ops[e]:
                toks.append((e, len(self.ops[e]) - 1))
        for t in self.last_dma.values():
            if t not in toks:
                toks.append(t)
        self.barrier = toks

    def _semkey(self, tok):
        (pe, pi) = tok
        p = self.ops[pe][pi]
        if p["is_dma"]:
            return (pe, "d", p["dsem"]), 16 * (p["dcnt"] + 1)
        return (pe, "c"), self.cnt_c[pe][pi + 1]

    def materialize(self):
        nc = self.nc
        for e in ENGINES:
            for op in self.ops[e]:
                if op["is_dma"]:
                    op["inc"] = True
                for (pe, pi) in op["deps"]:
                    self.ops[pe][pi]["inc"] = True
        self.cnt_c = {e: [0] * (len(self.ops[e]) + 1) for e in ENGINES}
        for e in ENGINES:
            cc = 0
            for i, op in enumerate(self.ops[e]):
                if op["inc"] and not op["is_dma"]:
                    cc += 1
                self.cnt_c[e][i + 1] = cc
        self._sem_ctx = []
        self.sems = {}
        for e in ENGINES:
            if self.cnt_c[e][-1] > 0:
                s = nc.semaphore(f"sem_{e}_c")
                self.sems[(e, "c")] = s.__enter__()
                self._sem_ctx.append(s)
            for si in range(self.NDSEM):
                if (e, si) in self.last_dma:
                    s = nc.semaphore(f"sem_{e}_d{si}")
                    self.sems[(e, "d", si)] = s.__enter__()
                    self._sem_ctx.append(s)

    def run_engine(self, e, eng_obj):
        waited = {}
        for op in self.ops[e]:
            needs = {}
            for tok in op["deps"]:
                key, v = self._semkey(tok)
                if v > needs.get(key, 0):
                    needs[key] = v
            if op["is_dma"] and op["dcnt"] > 0:
                key = (e, "d", op["dsem"])
                v = 16 * op["dcnt"]
                if v > needs.get(key, 0):
                    needs[key] = v
            for key, v in sorted(needs.items()):
                if v > waited.get(key, 0):
                    eng_obj.wait_ge(self.sems[key], v)
                    waited[key] = v
            instr = op["fn"](eng_obj)
            if op["inc"]:
                if op["is_dma"]:
                    instr.then_inc(self.sems[(e, "d", op["dsem"])], 16)
                else:
                    instr.then_inc(self.sems[(e, "c")], 1)

    def close(self):
        for s in reversed(self._sem_ctx):
            s.__exit__(None, None, None)


def _build():
    nc = bass.Bass("TRN2", target_bir_lowering=False)
    sch = Sched(nc)
    E, D = sch.emit, sch.dma

    def din(name, shape, dtype=F32):
        return nc.dram_tensor(name, list(shape), dtype, kind="ExternalInput").ap()

    def dscr(name, shape, dtype):
        kind = "ExternalOutput" if name in _DEBUG_OUT else "Internal"
        return nc.dram_tensor(name, list(shape), dtype, kind=kind).ap()

    x_d = din("x", (C, T))
    pcv_d = din("pcvec", (128, 26 * 8))
    alog_d = din("Alog", (128, 256))
    convw_d = din("convw", (128, 64))
    clips_d = din("clips", (2,))
    ident_d = din("ident", (128, 128), FP16)
    sel_d = din("sel", (64, 32 * 128), FP16)
    wk_d = din("Wk", (C, C)); wv_d = din("Wv", (C, C))
    wr_d = din("Wr", (C, C)); wo_d = din("Wo", (C, C))
    wck_d = din("Wck", (C, FF)); wcv_d = din("Wcv", (FF, C)); wcr_d = din("Wcr", (C, C))
    wg_d = din("wg_n", (II, C)); wu_d = din("wu_n", (II, C)); wd_d = din("wd_n", (C, II))
    inw_d = din("inwT", (2, DC, 2 * DI))
    xpw_d = din("xpwT", (2, DI, 64))
    dtw_d = din("dtwT", (2, 32, DI))
    mow_d = din("mowT", (2, DI, DC))
    pow_d = din("powT", (C, C))
    wfc_d = din("wfcT", (C, FF))
    wpr_d = din("wprT", (FF, C))

    out_d = nc.dram_tensor("out", [C, T], F32, kind="ExternalOutput").ap()

    xn_d = dscr("xn_s", (C, T), FP16)
    xmid_d = dscr("xmid_s", (C, T), FP16)
    s_d = dscr("s_s", (C, T), FP16)
    rr_d = dscr("rr_s", (C, T), FP16)
    rsq_d = dscr("rsq_s", (FF, T), FP16)
    gu_d = dscr("gu_s", (II, T), FP16)
    x1_d = dscr("x1_s", (C, T), FP16)
    s1_d = dscr("s1_s", (C, T), FP16)
    stats_d = dscr("stats_s", (16, T), F32)
    amax_d = dscr("amax_s", (128, 8), F32)
    cc_in1 = dscr("ccin1", (1,), F32)
    cc_out1 = dscr("ccout1", (1,), F32)
    cc_in2 = dscr("ccin2", (1,), F32)
    cc_out2 = dscr("ccout2", (1,), F32)

    def dview(ap, p=128):
        return ap.rearrange("(a p) t -> p a t", p=p)

    x_v = dview(x_d); xn_v = dview(xn_d); xmid_v = dview(xmid_d)
    s_v = dview(s_d); rr_v = dview(rr_d); rsq_v = dview(rsq_d)
    gu_v = dview(gu_d); x1_v = dview(x1_d); s1_v = dview(s1_d)
    out_v = dview(out_d)

    R_M, R_R0, R_R1, R_M1, R_R01, R_MF, R_R0F, R_RFC = 0, 1, 2, 3, 4, 5, 6, 7

    psum_ctx = []
    def psum(name, shape, dtype=F32):
        c = nc.psum_tensor(name, list(shape), dtype)
        t = c.__enter__()
        psum_ctx.append(c)
        return t
    PS = [psum(f"ps{i}", (128, 512)) for i in range(6)]
    pstat = psum("pstat", (128, 512))
    psT = psum("psT", (128, 1024), FP16)

    per = nc.auto_arena()
    pcv = per.sbuf_tensor("pcv", [128, 26 * 8], F32)
    alog = per.sbuf_tensor("alog_t", [128, 256], F32)
    aneg = per.sbuf_tensor("aneg", [128, 256], F32)
    convw = per.sbuf_tensor("convw_t", [128, 64], F32)
    ew_t = per.sbuf_tensor("ew_t", [128, 8], F32)
    eu_t = per.sbuf_tensor("eu_t", [128, 8], F32)
    ident = per.sbuf_tensor("ident_t", [128, 128], FP16)
    onesb = per.sbuf_tensor("onesb", [128, 1], FP16)
    ones1f = per.sbuf_tensor("ones1f", [1, 128], F32)
    srow = per.sbuf_tensor("srow", [1, 2048], F32)
    clp = per.sbuf_tensor("clp", [1, 2], F32)
    dcol = per.sbuf_tensor("dcol", [128, 1], F32)
    sc1 = per.sbuf_tensor("sc1", [128, 2], F32)
    sc2 = per.sbuf_tensor("sc2", [128, 2], F32)
    amaxa = per.sbuf_tensor("amaxa", [128, 8], F32)
    t15 = per.sbuf_tensor("t15", [128, 1], F32)
    i32d = per.sbuf_tensor("i32d", [128, 1], I32)
    tiny = per.sbuf_tensor("tinyws", [1, 16], F32)
    cbias = per.sbuf_tensor("cbias", [128, 4], F32)
    per.allocate()

    PC = dict(tm_k=0, tm_v=1, tm_r=2, td=3, tf=4, g0=5, b0=6, g1=7, b1=8,
              cm_k=9, cm_r=10, gL=11, bL=12, po_b=13, cb0=14, cb1=15,
              dtb0=16, dtb1=17, Dp0=18, Dp1=19, rms0=20, rms1=21)

    def pcol(slot, ct):
        return pcv[:, PC[slot] * 8 + ct: PC[slot] * 8 + ct + 1]

    def mm(ps_ap, lhsT, rhs, start, stop, reads, writes, deps=()):
        return E("tensor", (lambda e, o=ps_ap, l=lhsT, r=rhs, s1=start, s2=stop:
                            e.matmul(out=o, lhsT=l, rhs=r, start=s1, stop=s2)),
                 deps=deps, reads=reads, writes=writes)

    def act(out, in_, func, reads, writes, scale=None, bias=None, deps=()):
        kw = {}
        if scale is not None:
            kw["scale"] = scale
        if bias is not None:
            kw["bias"] = bias
        return E("scalar", (lambda e, o=out, i=in_, f=func, k=dict(kw):
                            e.activation(out=o, in_=i, func=f, **k)),
                 deps=deps, reads=reads, writes=writes)

    def tt(out, a, b, op, reads, writes, eng="vector", deps=()):
        return E(eng, (lambda e, o=out, x=a, y=b, p=op:
                       e.tensor_tensor(out=o, in0=x, in1=y, op=p)),
                 deps=deps, reads=reads, writes=writes)

    def ts(out, a, s1, s2, op0, op1, reads, writes, eng="vector", deps=()):
        return E(eng, (lambda e, o=out, x=a, u=s1, v=s2, p0=op0, p1=op1:
                       e.tensor_scalar(out=o, in0=x, scalar1=u, scalar2=v, op0=p0, op1=p1)),
                 deps=deps, reads=reads, writes=writes)

    def stt(out, in0, scalar, in1, op0, op1, reads, writes, deps=()):
        return E("vector", (lambda e, o=out, x=in0, s=scalar, y=in1, p0=op0, p1=op1:
                            e.scalar_tensor_tensor(out=o, in0=x, scalar=s, in1=y,
                                                   op0=p0, op1=p1)),
                 deps=deps, reads=reads, writes=writes)

    def scan(out, d0, d1, initial, reads, writes, deps=()):
        return E("vector", (lambda e, o=out, a=d0, b=d1, i=initial:
                            e.tensor_tensor_scan(out=o, data0=a, data1=b, initial=i,
                                                 op0=ALU.mult, op1=ALU.add)),
                 deps=deps, reads=reads, writes=writes)

    def vcopy(out, in_, reads, writes, eng="vector", deps=()):
        if eng == "scalar":
            return E(eng, (lambda e, o=out, i=in_: e.copy(out=o, in_=i)),
                     deps=deps, reads=reads, writes=writes)
        return E(eng, (lambda e, o=out, i=in_: e.tensor_copy(out=o, in_=i)),
                 deps=deps, reads=reads, writes=writes)

    def redmax(out, in_, reads, writes, deps=()):
        return E("vector", (lambda e, o=out, i=in_:
                            e.reduce_max(out=o, in_=i, axis=AX.X, apply_absolute_value=True)),
                 deps=deps, reads=reads, writes=writes)

    def memset(ap, val, writes, eng="gpsimd", deps=()):
        return E(eng, (lambda e, o=ap, v=val: e.memset(o, v)), deps=deps, writes=writes)

    def transpose(ps_ap, in_ap, reads, writes, deps=()):
        return E("tensor", (lambda e, o=ps_ap, i=in_ap:
                            e.transpose(out=o, in_=i, identity=ident[:, :])),
                 deps=deps, reads=list(reads) + ["ident"], writes=writes)

    def bcast_row(ps_ap, row_ap, writes, deps=(), reads=()):
        return E("tensor", (lambda e, o=ps_ap, r=row_ap:
                            e.matmul(out=o, lhsT=ones1f[:, :], rhs=r, start=True, stop=True)),
                 deps=deps, reads=list(reads) + ["srow", "ones1f"], writes=writes)

    def smul(out, in_, mulc, reads, writes):
        return E("scalar", (lambda e, o=out, i=in_, m=mulc: e.mul(out=o, in_=i, mul=m)),
                 reads=reads, writes=writes)

    D("sync", pcv[:, :], pcv_d, writes=["pcv"])
    D("sync", alog[:, :], alog_d, writes=["alog"])
    D("sync", convw[:, :], convw_d, writes=["convw"])
    D("sync", clp[0:1, 0:2], clips_d.rearrange("(a b) -> a b", a=1), writes=["clp"])
    D("sync", ident[:, :], ident_d, writes=["ident"])
    memset(cbias[:, 0:1], EPS, writes=["cbias"])
    memset(cbias[:, 1:2], 512.0, writes=["cbias"])
    memset(cbias[:, 2:3], 512.5, writes=["cbias"])
    memset(cbias[:, 3:4], 1.0, writes=["cbias"])
    memset(onesb[:, :], 1.0, writes=["onesb"])
    memset(amaxa[:, :], 0.0, writes=["amaxa"])
    memset(ones1f[:, :], 1.0, writes=["ones1f"])
    memset(t15[:, :], 1.5, writes=["t15"])
    act(i32d[:, :], t15[:, :], AF.Identity, reads=["t15"], writes=["i32d"])
    ts(dcol[:, :], i32d[:, :], 1.0, None, ALU.subtract, ALU.bypass,
       reads=["i32d"], writes=["dcol"])
    act(ew_t[:, :], pcv[:, PC["td"] * 8:PC["td"] * 8 + 8], AF.Exp, reads=["pcv"], writes=["ew"])
    act(ew_t[:, :], ew_t[:, :], AF.Exp, scale=-1.0, reads=["ew"], writes=["ew"])
    act(eu_t[:, :], pcv[:, PC["tf"] * 8:PC["tf"] * 8 + 8], AF.Exp, reads=["pcv"], writes=["eu"])
    act(aneg[:, :], alog[:, :], AF.Exp, reads=["alog"], writes=["aneg"])
    ts(aneg[:, :], aneg[:, :], -1.0, None, ALU.mult, ALU.bypass, reads=["aneg"], writes=["aneg"])

    def quant(out_bf, in_ap, iscale, i32a, i32b, tmpa, tmpb, kin, kout, ktmp):
        act(i32a, in_ap, AF.Identity, scale=iscale, bias=cbias[:, 1:2],
            reads=list(kin) + ["cbias"], writes=[ktmp + "a"])
        act(i32b, in_ap, AF.Identity, scale=iscale, bias=cbias[:, 2:3],
            reads=list(kin) + ["cbias"], writes=[ktmp + "b"])
        tt(tmpa, i32a, i32b, ALU.subtract, reads=[ktmp + "a", ktmp + "b"],
           writes=[ktmp + "t"])
        stt(tmpb, tmpa, dcol[:, 0:1], i32b, ALU.mult, ALU.add,
            reads=[ktmp + "t", ktmp + "b", "dcol"], writes=[ktmp + "d"])
        ts(tmpa, tmpb, 384.0, 639.0, ALU.max, ALU.min, reads=[ktmp + "d"],
           writes=[ktmp + "t"])
        ts(out_bf, tmpa, 512.0, None, ALU.subtract, ALU.bypass,
           reads=[ktmp + "t"], writes=[kout])

    def stats_rows(xb, sq, q, t0, kxb, ksq, rows, nch=C, nt=CT):
        for ct in range(nt):
            mm(pstat[0:1, 0:q], onesb[:, :], xb[:, ct, :], ct == 0, ct == nt - 1,
               reads=[kxb, "onesb"], writes=["pstat"])
        smul(srow[0:1, 0:q], pstat[0:1, 0:q], 1.0 / nch, reads=["pstat"], writes=["srow"])
        for ct in range(nt):
            mm(pstat[0:1, 0:q], onesb[:, :], sq[:, ct, :], ct == 0, ct == nt - 1,
               reads=[ksq, "onesb"], writes=["pstat"])
        act(srow[0:1, q:2 * q], srow[0:1, 0:q], AF.Square, reads=["srow"], writes=["srow"])
        stt(srow[0:1, q:2 * q], pstat[0:1, 0:q], 1.0 / nch, srow[0:1, q:2 * q],
            ALU.mult, ALU.subtract, reads=["pstat", "srow"], writes=["srow"])
        act(srow[0:1, 2 * q:3 * q], srow[0:1, q:2 * q], AF.Ln, bias=cbias[0:1, 0:1],
            reads=["srow", "cbias"], writes=["srow"])
        act(srow[0:1, 2 * q:3 * q], srow[0:1, 2 * q:3 * q], AF.Exp, scale=-0.5,
            reads=["srow"], writes=["srow"])
        for (rid, kind) in rows:
            if kind == "m":
                src = srow[0:1, 0:q]
            elif kind == "r0":
                src = srow[0:1, 2 * q:3 * q]
            else:
                act(srow[0:1, 3 * q:4 * q], srow[0:1, 2 * q:3 * q], AF.Square,
                    reads=["srow"], writes=["srow"])
                tt(srow[0:1, 3 * q:4 * q], srow[0:1, q:2 * q], srow[0:1, 3 * q:4 * q],
                   ALU.mult, reads=["srow"], writes=["srow"])
                act(srow[0:1, 3 * q:4 * q], srow[0:1, 3 * q:4 * q], AF.Ln,
                    bias=cbias[0:1, 0:1], reads=["srow", "cbias"], writes=["srow"])
                act(srow[0:1, 3 * q:4 * q], srow[0:1, 3 * q:4 * q], AF.Exp, scale=-0.5,
                    reads=["srow"], writes=["srow"])
                if kind == "rfc":
                    tt(srow[0:1, 3 * q:4 * q], srow[0:1, 3 * q:4 * q],
                       srow[0:1, 2 * q:3 * q], ALU.mult, reads=["srow"], writes=["srow"])
                src = srow[0:1, 3 * q:4 * q]
            D("sync", stats_d[rid:rid + 1, t0:t0 + q], src, reads=["srow"])

    def scale_pipeline(ccin, ccout, scol, clip_idx, skey):
        D("sync", amax_d, amaxa[:, :], reads=["amaxa"], writes=["amaxd"])
        D("sync", srow[0:1, 0:1024],
          amax_d.rearrange("p a -> (p a)").rearrange("(a b) -> a b", a=1),
          reads=["amaxd"], writes=["srow"])
        redmax(tiny[0:1, 0:1], srow[0:1, 0:1024], reads=["srow"], writes=["tiny"])
        D("gpsimd", ccin.rearrange("(a b) -> a b", a=1)[0:1, 0:1], tiny[0:1, 0:1],
          reads=["tiny"], writes=["ccin"])
        E("gpsimd", (lambda e, ci=ccin, co=ccout: e.collective_compute(
            "AllReduce", ALU.max, replica_groups=[list(range(8))],
            ins=[ci.opt()], outs=[co.opt()])),
          reads=["ccin"], writes=["ccbuf"])
        D("gpsimd", tiny[0:1, 1:2], ccout.rearrange("(a b) -> a b", a=1)[0:1, 0:1],
          reads=["ccbuf"], writes=["tiny"])
        E("scalar", lambda e, ci=clip_idx: e.activation(
            out=tiny[0:1, 2:3], in_=clp[0:1, ci:ci + 1], func=AF.Sigmoid),
          reads=["clp", "tiny"], writes=["tiny"])
        tt(tiny[0:1, 3:4], tiny[0:1, 1:2], tiny[0:1, 2:3], ALU.mult,
           reads=["tiny"], writes=["tiny"])
        ts(tiny[0:1, 3:4], tiny[0:1, 3:4], QMAXR, None, ALU.mult, ALU.bypass,
           reads=["tiny"], writes=["tiny"])
        act(tiny[0:1, 4:5], tiny[0:1, 3:4], AF.Ln, reads=["tiny"], writes=["tiny"])
        act(tiny[0:1, 4:5], tiny[0:1, 4:5], AF.Exp, scale=-1.0, reads=["tiny"], writes=["tiny"])
        E("tensor", (lambda e: e.matmul(out=PS[5][:, 0:2], lhsT=ones1f[:, :],
                                        rhs=tiny[0:1, 3:5], start=True, stop=True)),
          reads=["tiny", "ones1f"], writes=["ps5"])
        vcopy(scol[:, 0:2], PS[5][:, 0:2], reads=["ps5"], writes=[skey])

    def load_w_bf16(dst, dst_key, dram_view, kt_n, cols, stage, kstage, seg=1024):
        seg = min(cols, seg)
        for kt in range(kt_n):
            for s0 in range(0, cols, seg):
                D("gpsimd", stage[:, 0:seg], dram_view[:, kt, s0:s0 + seg], writes=[kstage])
                vcopy(dst[:, kt, s0:s0 + seg], stage[:, 0:seg], reads=[kstage],
                      writes=[dst_key], eng="gpsimd")

    if "A" in _PHASES:
        ar = nc.auto_arena()
        xch = ar.sbuf_tensor("A_x", [128, CT, QA], F32)
        xb = ar.sbuf_tensor("A_xb", [128, CT, QA], FP16)
        sq = ar.sbuf_tensor("A_sq", [128, CT, QA], FP16)
        xnb = ar.sbuf_tensor("A_xn", [128, CT, QA], FP16)
        amx = ar.sbuf_tensor("A_amx", [128, 1], F32)
        ar.allocate()
        for ci in range(T // QA):
            t0 = ci * QA
            D("sync", xch[:, :, :], x_v[:, :, t0:t0 + QA], writes=["A_x"])
            for ct in range(CT):
                act(xb[:, ct, :], xch[:, ct, :], AF.Identity, reads=["A_x"], writes=["A_xb"])
                act(sq[:, ct, :], xch[:, ct, :], AF.Square, reads=["A_x"], writes=["A_sq"])
            stats_rows(xb, sq, QA, t0, "A_xb", "A_sq",
                       [(R_M, "m"), (R_R0, "r0"), (R_R1, "r1")])
            bcast_row(PS[0][:, 0:QA], srow[0:1, 0:QA], writes=["ps0"])
            bcast_row(PS[1][:, 0:QA], srow[0:1, 2 * QA:3 * QA], writes=["ps1"])
            for ct in range(CT):
                tt(xch[:, ct, :], xch[:, ct, :], PS[0][:, 0:QA], ALU.subtract,
                   reads=["A_x", "ps0"], writes=["A_x"])
                tt(xnb[:, ct, :], xch[:, ct, :], PS[1][:, 0:QA], ALU.mult,
                   reads=["A_x", "ps1"], writes=["A_xn"])
                redmax(amx[:, 0:1], xnb[:, ct, :], reads=["A_xn"], writes=["A_amx"])
                if ci == 0 and ct == 0:
                    vcopy(amaxa[:, 0:1], amx[:, 0:1], reads=["A_amx"], writes=["amaxa"])
                else:
                    tt(amaxa[:, 0:1], amaxa[:, 0:1], amx[:, 0:1], ALU.max,
                       reads=["amaxa", "A_amx"], writes=["amaxa"])
            D("sync", xn_v[:, :, t0:t0 + QA], xnb[:, :, :], reads=["A_xn"])
        scale_pipeline(cc_in1, cc_out1, sc1, 0, "sxcol")
        sch.set_barrier()
        ar.free()

    if "B" in _PHASES:
        ar = nc.auto_arena()
        wK = ar.sbuf_tensor("B_wk", [128, CT, C], FP16)
        wV = ar.sbuf_tensor("B_wv", [128, CT, C], FP16)
        wR = ar.sbuf_tensor("B_wr", [128, CT, C], FP16)
        wO = ar.sbuf_tensor("B_wo", [128, CT, C], FP16)
        stg = ar.sbuf_tensor("B_stg", [128, 512], F32)
        xnb = ar.sbuf_tensor("B_xn", [128, 3, CT, QB], FP16)
        xah = ar.sbuf_tensor("B_xa", [128, CT, QB + 1], F32)
        dt_ = ar.sbuf_tensor("B_d", [128, CT, QB], FP16)
        mixR = ar.sbuf_tensor("B_mixR", [128, CT, QB], FP16)
        mixK = ar.sbuf_tensor("B_mixK", [128, CT, QB], FP16)
        mixV = ar.sbuf_tensor("B_mixV", [128, CT, QB], FP16)
        Et = ar.sbuf_tensor("B_E", [128, 2, CT, QB], F32)
        EVt = ar.sbuf_tensor("B_EV", [128, CT, QB], F32)
        NUMt = ar.sbuf_tensor("B_NUM", [128, CT, QB], F32)
        Ah = ar.sbuf_tensor("B_Ah", [128, CT, QB + 1], F32)
        Bh = ar.sbuf_tensor("B_Bh", [128, CT, QB + 1], F32)
        vps = ar.sbuf_tensor("B_vps", [128, 2, QB], F32)
        rsig = ar.sbuf_tensor("B_rs", [128, 2, CT, QB], FP16)
        rw = ar.sbuf_tensor("B_rw", [128, 2, CT, QB], FP16)
        xm = ar.sbuf_tensor("B_xm", [128, 2, QB], F32)
        xm16 = ar.sbuf_tensor("B_xm16", [128, CT, QB], FP16)
        sq16 = ar.sbuf_tensor("B_sq16", [128, CT, QB], FP16)
        ewb = ar.sbuf_tensor("B_ewb", [128, CT, QB], F32)
        ar.allocate()

        load_w_bf16(wK, "B_wk", dview(wk_d), CT, C, stg, "B_stg", seg=512)
        load_w_bf16(wV, "B_wv", dview(wv_d), CT, C, stg, "B_stg", seg=512)
        load_w_bf16(wR, "B_wr", dview(wr_d), CT, C, stg, "B_stg", seg=512)
        load_w_bf16(wO, "B_wo", dview(wo_d), CT, C, stg, "B_stg", seg=512)
        memset(ewb[:, :, :], 1.0, writes=["B_ewb"])
        for ct in range(CT):
            ts(ewb[:, ct, :], ewb[:, ct, :], ew_t[:, ct:ct + 1], None, ALU.mult, ALU.bypass,
               reads=["B_ewb", "ew"], writes=["B_ewb"])
        memset(xah[:, :, 0:1], 0.0, writes=["B_xa"])
        memset(Ah[:, :, 0:1], 0.0, writes=["B_Ah"])
        memset(Bh[:, :, 0:1], 0.0, writes=["B_Bh"])

        NCH = T // QB

        def b_load(ci):
            p = ci % 2
            p3 = ci % 3
            t0 = ci * QB
            D("sync", xnb[:, p3, :, :], xn_v[:, :, t0:t0 + QB], writes=[f"B_xn{p3}"])
            D("sync", srow[0:1, 1024 + p * QB:1024 + p * QB + QB],
              stats_d[R_R1:R_R1 + 1, t0:t0 + QB], writes=[f"B_srow{p}"])

        def b_tail(cj):
            p = cj % 2
            p3 = cj % 3
            t0 = cj * QB
            for ot in range(CT):
                q = ot % 2
                pb = PS[2 + q]
                for ct in range(CT):
                    mm(pb[:, 0:QB], wO[:, ct, ot * 128:(ot + 1) * 128], rw[:, p, ct, :],
                       ct == 0, ct == CT - 1, reads=["B_wo", f"B_rw{p}"],
                       writes=[f"pso{q}"])
                tt(xm[:, q, :], xnb[:, p3, ot, :], pb[:, 0:QB], ALU.add,
                   reads=[f"B_xn{p3}", f"pso{q}"], writes=[f"B_xm{q}"])
                act(xm16[:, ot, :], xm[:, q, :], AF.Identity, reads=[f"B_xm{q}"],
                    writes=["B_xm16"])
                act(sq16[:, ot, :], xm[:, q, :], AF.Square, reads=[f"B_xm{q}"],
                    writes=["B_sq16"])
            stats_rows(xm16, sq16, QB, t0, "B_xm16", "B_sq16",
                       [(R_M1, "m"), (R_R01, "r0")])
            D("gpsimd", xmid_v[:, :, t0:t0 + QB], xm16[:, :, :], reads=["B_xm16"])

        b_load(0)
        b_load(1)
        for ci in range(NCH):
            p = ci % 2
            p3 = ci % 3
            E("tensor", (lambda e, p=p: e.matmul(
                out=PS[5][:, 256:512], lhsT=ones1f[:, :],
                rhs=srow[0:1, 1024 + p * QB:1024 + p * QB + QB], start=True, stop=True)),
              reads=[f"B_srow{p}", "ones1f"], writes=["psbc", "psv1"])
            for ct in range(CT):
                tt(xah[:, ct, 1:QB + 1], xnb[:, p3, ct, :], PS[5][:, 256:512],
                   ALU.mult, reads=[f"B_xn{p3}", "psbc"], writes=["B_xa"])
                ts(xah[:, ct, 1:QB + 1], xah[:, ct, 1:QB + 1], pcol("g0", ct), pcol("b0", ct),
                   ALU.mult, ALU.add, reads=["B_xa", "pcv"], writes=["B_xa"])
                tt(dt_[:, ct, :], xah[:, ct, 1:QB + 1], xah[:, ct, 0:QB], ALU.subtract,
                   reads=["B_xa"], writes=["B_d"])
                stt(mixR[:, ct, :], dt_[:, ct, :], pcol("tm_r", ct), xah[:, ct, 0:QB],
                    ALU.mult, ALU.add, reads=["B_d", "B_xa", "pcv"], writes=["B_mixR"])
                stt(mixK[:, ct, :], dt_[:, ct, :], pcol("tm_k", ct), xah[:, ct, 0:QB],
                    ALU.mult, ALU.add, reads=["B_d", "B_xa", "pcv"], writes=["B_mixK"])
                stt(mixV[:, ct, :], dt_[:, ct, :], pcol("tm_v", ct), xah[:, ct, 0:QB],
                    ALU.mult, ALU.add, reads=["B_d", "B_xa", "pcv"], writes=["B_mixV"])
            vcopy(xah[:, :, 0:1], xah[:, :, QB:QB + 1], reads=["B_xa"], writes=["B_xa"])
            for ot in range(CT):
                q = ot % 2
                pb = PS[2 + q]
                for ct in range(CT):
                    mm(pb[:, 0:QB], wR[:, ct, ot * 128:(ot + 1) * 128], mixR[:, ct, :],
                       ct == 0, ct == CT - 1, reads=["B_wr", "B_mixR"], writes=[f"pso{q}"])
                act(rsig[:, p, ot, :], pb[:, 0:QB], AF.Sigmoid, reads=[f"pso{q}"],
                    writes=[f"B_rs{p}"])
            for ot in range(CT):
                q = ot % 2
                pb = PS[q]
                for ct in range(CT):
                    mm(pb[:, 0:QB], wK[:, ct, ot * 128:(ot + 1) * 128], mixK[:, ct, :],
                       ct == 0, ct == CT - 1, reads=["B_wk", "B_mixK"], writes=[f"psk{q}"])
                act(Et[:, p, ot, :], pb[:, 0:QB], AF.Exp, reads=[f"psk{q}"],
                    writes=[f"B_E{p}_{ot}"])
                scan(Bh[:, ot, 1:QB + 1], ewb[:, ot, :], Et[:, p, ot, :], Bh[:, ot, 0:1],
                     reads=["B_ewb", f"B_E{p}_{ot}", "B_Bh"], writes=["B_Bh"])
            for ot in range(CT):
                q = ot % 2
                pb = PS[4 + q]
                for ct in range(CT):
                    mm(pb[:, 0:QB], wV[:, ct, ot * 128:(ot + 1) * 128], mixV[:, ct, :],
                       ct == 0, ct == CT - 1, reads=["B_wv", "B_mixV"], writes=[f"psv{q}"])
                vcopy(vps[:, q, :], pb[:, 0:QB], reads=[f"psv{q}"], writes=[f"B_vps{q}"],
                      eng="scalar")
                tt(EVt[:, ot, :], Et[:, p, ot, :], vps[:, q, :], ALU.mult,
                   reads=[f"B_E{p}_{ot}", f"B_vps{q}"], writes=[f"B_EV{ot}"])
                act(Et[:, p, ot, :], Et[:, p, ot, :], AF.Identity, scale=eu_t[:, ot:ot + 1],
                    reads=[f"B_E{p}_{ot}", "eu"], writes=[f"B_E{p}_{ot}"])
                tt(NUMt[:, ot, :], Et[:, p, ot, :], vps[:, q, :], ALU.mult,
                   reads=[f"B_E{p}_{ot}", f"B_vps{q}"], writes=[f"B_NUM{ot}"])
                scan(Ah[:, ot, 1:QB + 1], ewb[:, ot, :], EVt[:, ot, :], Ah[:, ot, 0:1],
                     reads=["B_ewb", f"B_EV{ot}", "B_Ah"], writes=["B_Ah"])
                tt(NUMt[:, ot, :], NUMt[:, ot, :], Ah[:, ot, 0:QB], ALU.add,
                   reads=[f"B_NUM{ot}", "B_Ah"], writes=[f"B_NUM{ot}"])
                tt(EVt[:, ot, :], Bh[:, ot, 0:QB], Et[:, p, ot, :], ALU.add,
                   reads=["B_Bh", f"B_E{p}_{ot}"], writes=[f"B_EV{ot}"])
                act(EVt[:, ot, :], EVt[:, ot, :], AF.Ln, reads=[f"B_EV{ot}"],
                    writes=[f"B_EV{ot}"])
                act(EVt[:, ot, :], EVt[:, ot, :], AF.Exp, scale=-1.0,
                    reads=[f"B_EV{ot}"], writes=[f"B_EV{ot}"])
                tt(NUMt[:, ot, :], NUMt[:, ot, :], EVt[:, ot, :], ALU.mult,
                   reads=[f"B_NUM{ot}", f"B_EV{ot}"], writes=[f"B_NUM{ot}"])
                tt(rw[:, p, ot, :], NUMt[:, ot, :], rsig[:, p, ot, :], ALU.mult,
                   reads=[f"B_NUM{ot}", f"B_rs{p}"], writes=[f"B_rw{p}"])
            vcopy(Ah[:, :, 0:1], Ah[:, :, QB:QB + 1], reads=["B_Ah"], writes=["B_Ah"])
            vcopy(Bh[:, :, 0:1], Bh[:, :, QB:QB + 1], reads=["B_Bh"], writes=["B_Bh"])
            if ci > 0:
                b_tail(ci - 1)
            if ci + 2 < NCH:
                b_load(ci + 2)
        b_tail(NCH - 1)
        sch.set_barrier()
        ar.free()

    if "C1" in _PHASES:
        ar = nc.auto_arena()
        wCK = ar.sbuf_tensor("C_wck", [128, CT, FF], FP16)
        wCR = ar.sbuf_tensor("C_wcr", [128, CT, C], FP16)
        stg = ar.sbuf_tensor("C_stg", [128, 1024], F32)
        xmb = ar.sbuf_tensor("C_xm", [128, 2, CT, QC], FP16)
        xch = ar.sbuf_tensor("C_xc", [128, CT, QC + 1], F32)
        dt_ = ar.sbuf_tensor("C_d", [128, CT, QC], FP16)
        mk = ar.sbuf_tensor("C_mk", [128, 2, CT, QC], FP16)
        mr = ar.sbuf_tensor("C_mr", [128, 2, CT, QC], FP16)
        rl = ar.sbuf_tensor("C_rl", [128, 2, QC], FP16)
        rq = ar.sbuf_tensor("C_rq", [128, 2, QC], FP16)
        rrt = ar.sbuf_tensor("C_rr", [128, 2, CT, QC], FP16)
        ar.allocate()
        load_w_bf16(wCK, "C_wck", dview(wck_d), CT, FF, stg, "C_stg")
        load_w_bf16(wCR, "C_wcr", dview(wcr_d), CT, C, stg, "C_stg")
        memset(xch[:, :, 0:1], 0.0, writes=["C_xc"])
        NCH = T // QC

        def c_load(ci):
            p = ci % 2
            t0 = ci * QC
            D("sync", xmb[:, p, :, :], xmid_v[:, :, t0:t0 + QC], writes=[f"C_xm{p}"])
            D("sync", srow[0:1, p * 1024:p * 1024 + QC],
              stats_d[R_M1:R_M1 + 1, t0:t0 + QC], writes=[f"C_srow{p}"])
            D("sync", srow[0:1, p * 1024 + QC:p * 1024 + 2 * QC],
              stats_d[R_R01:R_R01 + 1, t0:t0 + QC], writes=[f"C_srow{p}"])

        c_load(0)
        c_load(1)
        for ci in range(NCH):
            t0 = ci * QC
            p = ci % 2
            bcast_row(PS[3][:, 0:QC], srow[0:1, p * 1024:p * 1024 + QC],
                      writes=["ps3"], reads=[f"C_srow{p}"])
            bcast_row(PS[4][:, 0:QC], srow[0:1, p * 1024 + QC:p * 1024 + 2 * QC],
                      writes=["ps4"], reads=[f"C_srow{p}"])
            for ct in range(CT):
                tt(xch[:, ct, 1:QC + 1], xmb[:, p, ct, :], PS[3][:, 0:QC], ALU.subtract,
                   reads=[f"C_xm{p}", "ps3"], writes=["C_xc"])
                tt(xch[:, ct, 1:QC + 1], xch[:, ct, 1:QC + 1], PS[4][:, 0:QC], ALU.mult,
                   reads=["C_xc", "ps4"], writes=["C_xc"])
                ts(xch[:, ct, 1:QC + 1], xch[:, ct, 1:QC + 1], pcol("g1", ct), pcol("b1", ct),
                   ALU.mult, ALU.add, reads=["C_xc", "pcv"], writes=["C_xc"])
                tt(dt_[:, ct, :], xch[:, ct, 1:QC + 1], xch[:, ct, 0:QC], ALU.subtract,
                   reads=["C_xc"], writes=["C_d"])
                stt(mk[:, p, ct, :], dt_[:, ct, :], pcol("cm_k", ct), xch[:, ct, 0:QC],
                    ALU.mult, ALU.add, reads=["C_d", "C_xc", "pcv"], writes=[f"C_mk{p}"])
                stt(mr[:, p, ct, :], dt_[:, ct, :], pcol("cm_r", ct), xch[:, ct, 0:QC],
                    ALU.mult, ALU.add, reads=["C_d", "C_xc", "pcv"], writes=[f"C_mr{p}"])
            vcopy(xch[:, :, 0:1], xch[:, :, QC:QC + 1], reads=["C_xc"], writes=["C_xc"])
            if ci + 2 < NCH:
                c_load(ci + 2)
            for ft in range(FT):
                q = ft % 2
                pb = PS[q]
                for ct in range(CT):
                    mm(pb[:, 0:QC], wCK[:, ct, ft * 128:(ft + 1) * 128], mk[:, p, ct, :],
                       ct == 0, ct == CT - 1, reads=["C_wck", f"C_mk{p}"],
                       writes=[f"psk{q}"])
                act(rl[:, q, :], pb[:, 0:QC], AF.Relu, reads=[f"psk{q}"],
                    writes=[f"C_rl{q}"])
                tt(rq[:, q, :], rl[:, q, :], rl[:, q, :], ALU.mult, reads=[f"C_rl{q}"],
                   writes=[f"C_rq{q}"])
                D("gpsimd", rsq_v[:, ft, t0:t0 + QC], rq[:, q, :], reads=[f"C_rq{q}"])
            for ot in range(CT):
                q = ot % 2
                pb = PS[2] if q == 0 else PS[5]
                for ct in range(CT):
                    mm(pb[:, 0:QC], wCR[:, ct, ot * 128:(ot + 1) * 128], mr[:, p, ct, :],
                       ct == 0, ct == CT - 1, reads=["C_wcr", f"C_mr{p}"],
                       writes=[f"psr{q}"])
                act(rrt[:, p, ot, :], pb[:, 0:QC], AF.Sigmoid, reads=[f"psr{q}"],
                    writes=[f"C_rr{p}"])
            D("gpsimd", rr_v[:, :, t0:t0 + QC], rrt[:, p, :, :], reads=[f"C_rr{p}"])
        sch.set_barrier()
        ar.free()

    if "C2" in _PHASES:
        ar = nc.auto_arena()
        wCV = ar.sbuf_tensor("C2_wcv", [128, FT, C], FP16)
        stg = ar.sbuf_tensor("C2_stg", [128, C], F32)
        rsq = ar.sbuf_tensor("C2_rsq", [128, 2, FT, QC], FP16)
        xmb = ar.sbuf_tensor("C2_xm", [128, 2, CT, QC], FP16)
        rrt = ar.sbuf_tensor("C2_rr", [128, 2, CT, QC], FP16)
        st = ar.sbuf_tensor("C2_s", [128, 2, CT, QC], FP16)
        tmp = ar.sbuf_tensor("C2_t", [128, 2, QC], F32)
        ar.allocate()
        load_w_bf16(wCV, "C2_wcv", dview(wcv_d), FT, C, stg, "C2_stg")
        NCH = T // QC

        def c2_load(ci):
            p = ci % 2
            t0 = ci * QC
            D("sync", rsq[:, p, :, :], rsq_v[:, :, t0:t0 + QC], writes=[f"C2_rsq{p}"])
            D("sync", xmb[:, p, :, :], xmid_v[:, :, t0:t0 + QC], writes=[f"C2_xm{p}"])
            D("sync", rrt[:, p, :, :], rr_v[:, :, t0:t0 + QC], writes=[f"C2_rr{p}"])

        c2_load(0)
        c2_load(1)
        for ci in range(NCH):
            t0 = ci * QC
            p = ci % 2
            for ot in range(CT):
                q = ot % 2
                pb = PS[q]
                for ft in range(FT):
                    mm(pb[:, 0:QC], wCV[:, ft, ot * 128:(ot + 1) * 128], rsq[:, p, ft, :],
                       ft == 0, ft == FT - 1, reads=["C2_wcv", f"C2_rsq{p}"],
                       writes=[f"psc{q}"])
                tt(tmp[:, q, :], rrt[:, p, ot, :], pb[:, 0:QC], ALU.mult,
                   reads=[f"C2_rr{p}", f"psc{q}"], writes=[f"C2_t{q}"])
                tt(st[:, p, ot, :], xmb[:, p, ot, :], tmp[:, q, :], ALU.add,
                   reads=[f"C2_xm{p}", f"C2_t{q}"], writes=[f"C2_s{p}"])
            D("gpsimd", s_v[:, :, t0:t0 + QC], st[:, p, :, :], reads=[f"C2_s{p}"])
            if ci + 2 < NCH:
                c2_load(ci + 2)
        sch.set_barrier()
        ar.free()

    if "D" in _PHASES:
        ar = nc.auto_arena()
        wgT = ar.sbuf_tensor("D_wgT", [128, CT, II], FP16)
        wuT = ar.sbuf_tensor("D_wuT", [128, CT, II], FP16)
        stg = ar.sbuf_tensor("D_stg", [128, 2, C], F32)
        i32a = ar.sbuf_tensor("D_i32a", [128, C], I32)
        i32b = ar.sbuf_tensor("D_i32b", [128, C], I32)
        tma = ar.sbuf_tensor("D_tma", [128, C], F32)
        tmb = ar.sbuf_tensor("D_tmb", [128, C], F32)
        qw = ar.sbuf_tensor("D_qw", [128, 2, C], FP16)
        sg_s = ar.sbuf_tensor("D_sg", [128, IT], F32)
        su_s = ar.sbuf_tensor("D_su", [128, IT], F32)
        inv_s = ar.sbuf_tensor("D_inv", [128, 1], F32)
        osg = ar.sbuf_tensor("D_osg", [128, IT], F32)
        osu = ar.sbuf_tensor("D_osu", [128, IT], F32)
        xnb = ar.sbuf_tensor("D_xn", [128, 2, CT, QD], FP16)
        xq = ar.sbuf_tensor("D_xq", [128, 2, CT, QD], FP16)
        qa = ar.sbuf_tensor("D_qa", [128, 2, QD], I32)
        qb = ar.sbuf_tensor("D_qb", [128, 2, QD], I32)
        qta = ar.sbuf_tensor("D_qta", [128, 2, QD], F32)
        qtb = ar.sbuf_tensor("D_qtb", [128, 2, QD], F32)
        sgt = ar.sbuf_tensor("D_sgt", [128, 2, QD], FP16)
        upt = ar.sbuf_tensor("D_upt", [128, 2, QD], FP16)
        gut = ar.sbuf_tensor("D_gut", [128, 2, QD], FP16)
        amx = ar.sbuf_tensor("D_amx", [128, 1], F32)
        ar.allocate()

        for (w_dram, scol, skey, wT, wTkey) in (
                (wg_d, sg_s, "D_sg", wgT, "D_wgT"),
                (wu_d, su_s, "D_su", wuT, "D_wuT")):
            wv22 = dview(w_dram)
            for it in range(IT if _D_NIT is None else _D_NIT):
                p = it % 2
                D("gpsimd" if p else "sync", stg[:, p, :], wv22[:, it, :],
                  writes=[f"D_stg{p}"])
                redmax(scol[:, it:it + 1], stg[:, p, :], reads=[f"D_stg{p}"],
                       writes=[skey])
                ts(inv_s[:, 0:1], scol[:, it:it + 1], QMAXR, None, ALU.mult, ALU.bypass,
                   reads=[skey], writes=["D_inv"])
                act(inv_s[:, 0:1], inv_s[:, 0:1], AF.Ln, reads=["D_inv"], writes=["D_inv"])
                act(inv_s[:, 0:1], inv_s[:, 0:1], AF.Exp, scale=-1.0,
                    reads=["D_inv"], writes=["D_inv"])
                quant(qw[:, p, :], stg[:, p, :], inv_s[:, 0:1], i32a[:, 0:C], i32b[:, 0:C],
                      tma[:, 0:C], tmb[:, 0:C], [f"D_stg{p}", "D_inv"], f"D_qw{p}", "D_q")
                for kt in range(CT):
                    transpose(psT[:, 0:128], qw[:, p, kt * 128:(kt + 1) * 128],
                              reads=[f"D_qw{p}"], writes=["psTq"])
                    vcopy(wT[:, kt, it * 128:(it + 1) * 128], psT[:, 0:128],
                          reads=["psTq"], writes=[wTkey], eng="scalar")
            ts(scol[:, 0:IT], scol[:, 0:IT], QMAXR, None, ALU.mult, ALU.bypass,
               reads=[skey], writes=[skey])
        ts(osg[:, 0:IT], sg_s[:, 0:IT], sc1[:, 0:1], None, ALU.mult, ALU.bypass,
           reads=["D_sg", "sxcol"], writes=["D_osg"])
        ts(osu[:, 0:IT], su_s[:, 0:IT], sc1[:, 0:1], None, ALU.mult, ALU.bypass,
           reads=["D_su", "sxcol"], writes=["D_osu"])

        NCH = T // QD

        def d_load(ci):
            p = ci % 2
            D("sync", xnb[:, p, :, :], xn_v[:, :, ci * QD:(ci + 1) * QD],
              writes=[f"D_xn{p}"])

        def d_quant(ci):
            p = ci % 2
            for ct in range(CT):
                q = ct % 2
                quant(xq[:, p, ct, :], xnb[:, p, ct, :], sc1[:, 1:2],
                      qa[:, q, :], qb[:, q, :], qta[:, q, :], qtb[:, q, :],
                      [f"D_xn{p}", "sxcol"], f"D_xq{p}", f"D_x{q}")

        d_load(0)
        d_quant(0)
        d_load(1)
        for ci in range(0 if _D_TRUNC == 1 else NCH):
            t0 = ci * QD
            p = ci % 2
            if ci + 1 < NCH:
                d_quant(ci + 1)
            if ci + 2 < NCH:
                d_load(ci + 2)
            for it in range(IT):
                q = it % 2
                pg, pu = 2 * q, 2 * q + 1
                for ct in range(CT):
                    mm(PS[pg][:, 0:QD], wgT[:, ct, it * 128:(it + 1) * 128],
                       xq[:, p, ct, :], ct == 0, ct == CT - 1,
                       reads=["D_wgT", f"D_xq{p}"], writes=[f"ps{pg}"])
                for ct in range(CT):
                    mm(PS[pu][:, 0:QD], wuT[:, ct, it * 128:(it + 1) * 128],
                       xq[:, p, ct, :], ct == 0, ct == CT - 1,
                       reads=["D_wuT", f"D_xq{p}"], writes=[f"ps{pu}"])
                act(sgt[:, q, :], PS[pg][:, 0:QD], AF.Silu, scale=osg[:, it:it + 1],
                    reads=[f"ps{pg}", "D_osg"], writes=[f"D_sgt{q}"])
                act(upt[:, q, :], PS[pu][:, 0:QD], AF.Identity, scale=osu[:, it:it + 1],
                    reads=[f"ps{pu}", "D_osu"], writes=[f"D_upt{q}"])
                tt(gut[:, q, :], sgt[:, q, :], upt[:, q, :], ALU.mult,
                   reads=[f"D_sgt{q}", f"D_upt{q}"], writes=[f"D_gut{q}"])
                redmax(amx[:, 0:1], gut[:, q, :], reads=[f"D_gut{q}"], writes=["D_amx"])
                if ci == 0 and it == 0:
                    vcopy(amaxa[:, 0:1], amx[:, 0:1], reads=["D_amx"], writes=["amaxa"])
                else:
                    tt(amaxa[:, 0:1], amaxa[:, 0:1], amx[:, 0:1], ALU.max,
                       reads=["amaxa", "D_amx"], writes=["amaxa"])
                D("gpsimd", gu_v[:, it, t0:t0 + QD], gut[:, q, :], reads=[f"D_gut{q}"])
        if _D_TRUNC == 0:
            scale_pipeline(cc_in2, cc_out2, sc2, 1, "sdcol")
        sch.set_barrier()
        ar.free()

    if "E" in _PHASES:
        ar = nc.auto_arena()
        wdT = ar.sbuf_tensor("E_wdT", [128, IT, C], FP16)
        stg = ar.sbuf_tensor("E_stg", [128, II], F32)
        i32a = ar.sbuf_tensor("E_i32a", [128, 704], I32)
        i32b = ar.sbuf_tensor("E_i32b", [128, 704], I32)
        tma = ar.sbuf_tensor("E_tma", [128, 704], F32)
        tmb = ar.sbuf_tensor("E_tmb", [128, 704], F32)
        qw = ar.sbuf_tensor("E_qw", [128, II], FP16)
        sd_s = ar.sbuf_tensor("E_sd", [128, CT], F32)
        inv_s = ar.sbuf_tensor("E_inv", [128, 1], F32)
        osd = ar.sbuf_tensor("E_osd", [128, CT], F32)
        gub = ar.sbuf_tensor("E_gu", [128, 2, IT, QE], FP16)
        qa = ar.sbuf_tensor("E_qa", [128, QE], I32)
        qb = ar.sbuf_tensor("E_qb", [128, QE], I32)
        qta = ar.sbuf_tensor("E_qta", [128, QE], F32)
        qtb = ar.sbuf_tensor("E_qtb", [128, QE], F32)
        xch = ar.sbuf_tensor("E_x", [128, CT, QE], F32)
        sb = ar.sbuf_tensor("E_s", [128, 2, CT, QE], FP16)
        gt = ar.sbuf_tensor("E_g", [128, 2, QE], FP16)
        tb = ar.sbuf_tensor("E_t", [128, 2, QE], FP16)
        x1t = ar.sbuf_tensor("E_x1", [128, 2, QE], F32)
        xb16 = ar.sbuf_tensor("E_xb", [128, CT, QE], FP16)
        sq16 = ar.sbuf_tensor("E_sq", [128, CT, QE], FP16)
        ar.allocate()

        wv8 = dview(wd_d)
        for ht in range(CT):
            p = ht % 2
            D("gpsimd" if p else "sync", stg[:, :], wv8[:, ht, :],
              writes=["E_stg"])
            redmax(sd_s[:, ht:ht + 1], stg[:, :], reads=["E_stg"], writes=["E_sd"])
            ts(inv_s[:, 0:1], sd_s[:, ht:ht + 1], QMAXR, None, ALU.mult, ALU.bypass,
               reads=["E_sd"], writes=["E_inv"])
            act(inv_s[:, 0:1], inv_s[:, 0:1], AF.Ln, reads=["E_inv"], writes=["E_inv"])
            act(inv_s[:, 0:1], inv_s[:, 0:1], AF.Exp, scale=-1.0,
                reads=["E_inv"], writes=["E_inv"])
            for sg0 in range(0, II, 704):
                quant(qw[:, sg0:sg0 + 704], stg[:, sg0:sg0 + 704], inv_s[:, 0:1],
                      i32a[:, :], i32b[:, :], tma[:, :], tmb[:, :],
                      ["E_stg", "E_inv"], "E_qw", "E_q")
            for it in range(IT):
                transpose(psT[:, 0:128], qw[:, it * 128:(it + 1) * 128],
                          reads=["E_qw"], writes=["psTq"])
                vcopy(wdT[:, it, ht * 128:(ht + 1) * 128], psT[:, 0:128],
                      reads=["psTq"], writes=["E_wdT"], eng="scalar")
        ts(sd_s[:, 0:CT], sd_s[:, 0:CT], QMAXR, None, ALU.mult, ALU.bypass,
           reads=["E_sd"], writes=["E_sd"])
        ts(osd[:, 0:CT], sd_s[:, 0:CT], sc2[:, 0:1], None, ALU.mult, ALU.bypass,
           reads=["E_sd", "sdcol"], writes=["E_osd"])

        NCH = T // QE

        def e_load(ci):
            p = ci % 2
            t0 = ci * QE
            D("sync", gub[:, p, :, :], gu_v[:, :, t0:t0 + QE], writes=[f"E_gu{p}"])
            D("sync", sb[:, p, :, :], s_v[:, :, t0:t0 + QE], writes=[f"E_s{p}"])

        def e_quant(ci):
            p = ci % 2
            for it in range(IT):
                quant(gub[:, p, it, :], gub[:, p, it, :], sc2[:, 1:2],
                      qa[:, :], qb[:, :], qta[:, :], qtb[:, :],
                      [f"E_gu{p}", "sdcol"], f"E_gu{p}", "E_q2")

        e_load(0)
        e_quant(0)
        e_load(1)
        for ci in range(NCH):
            t0 = ci * QE
            p = ci % 2
            D("sync", xch[:, :, :], x_v[:, :, t0:t0 + QE], writes=["E_x"])
            if ci + 1 < NCH:
                e_quant(ci + 1)
            for ht in range(CT):
                q = ht % 2
                pb = PS[q]
                for it in range(IT):
                    mm(pb[:, 0:QE], wdT[:, it, ht * 128:(ht + 1) * 128],
                       gub[:, p, it, :], it == 0, it == IT - 1,
                       reads=["E_wdT", f"E_gu{p}"], writes=[f"pse{q}"])
                act(gt[:, q, :], pb[:, 0:QE], AF.Silu, scale=osd[:, ht:ht + 1],
                    reads=[f"pse{q}", "E_osd"], writes=[f"E_g{q}"])
                tt(tb[:, q, :], sb[:, p, ht, :], gt[:, q, :], ALU.mult,
                   reads=[f"E_s{p}", f"E_g{q}"], writes=[f"E_t{q}"])
                tt(x1t[:, q, :], xch[:, ht, :], tb[:, q, :], ALU.add,
                   reads=["E_x", f"E_t{q}"], writes=[f"E_x1{q}"])
                act(xb16[:, ht, :], x1t[:, q, :], AF.Identity, reads=[f"E_x1{q}"],
                    writes=["E_xb"])
                act(sq16[:, ht, :], x1t[:, q, :], AF.Square, reads=[f"E_x1{q}"],
                    writes=["E_sq"])
            stats_rows(xb16, sq16, QE, t0, "E_xb", "E_sq",
                       [(R_MF, "m"), (R_R0F, "r0"), (R_RFC, "rfc")])
            D("gpsimd", x1_v[:, :, t0:t0 + QE], xb16[:, :, :], reads=["E_xb"])
            if ci + 2 < NCH:
                e_load(ci + 2)
        sch.set_barrier()
        ar.free()

    if "F1" in _PHASES:
        ar = nc.auto_arena()
        wIN = [ar.sbuf_tensor(f"F_win{l}", [128, 4, 2 * DI], FP16) for l in range(2)]
        wXP = [ar.sbuf_tensor(f"F_wxp{l}", [128, CT, 64], FP16) for l in range(2)]
        wDT = [ar.sbuf_tensor(f"F_wdt{l}", [32, DI], FP16) for l in range(2)]
        wMO = [ar.sbuf_tensor(f"F_wmo{l}", [128, CT, DC], FP16) for l in range(2)]
        wPO = ar.sbuf_tensor("F_wpo", [128, CT, C], FP16)
        stg = ar.sbuf_tensor("F_stg", [128, 512], F32)
        sL16 = ar.sbuf_tensor("F_sL16", [128, CT, QF1], FP16)
        sL = ar.sbuf_tensor("F_sL", [128, CT, QF1], FP16)
        hh = ar.sbuf_tensor("F_h", [128, 4, QF1], F32)
        hn = ar.sbuf_tensor("F_hn", [128, 4, QF1], FP16)
        hsq = ar.sbuf_tensor("F_hsq", [128, 4, QF1], FP16)
        xc = [ar.sbuf_tensor(f"F_xc{l}", [128, CT, QF1 + 3], FP16) for l in range(2)]
        zs = ar.sbuf_tensor("F_z", [128, CT, QF1], FP16)
        cvt = ar.sbuf_tensor("F_cv", [128, QF1], F32)
        xcs = ar.sbuf_tensor("F_xcs", [128, CT, QF1], FP16)
        dbc = ar.sbuf_tensor("F_dbc", [64, QF1], FP16)
        delt = ar.sbuf_tensor("F_del", [128, CT, QF1], FP16)
        ut = ar.sbuf_tensor("F_u", [128, CT, QF1], FP16)
        yac = ar.sbuf_tensor("F_yac", [128, CT, QF1], FP16)
        bmb = ar.sbuf_tensor("F_bmb", [128, 2, QF1], FP16)
        dA = ar.sbuf_tensor("F_dA", [128, 2, QF1], FP16)
        dBx = ar.sbuf_tensor("F_dBx", [128, 2, QF1], FP16)
        ht = ar.sbuf_tensor("F_ht", [128, 2, QF1], FP16)
        carr = [ar.sbuf_tensor(f"F_carr{l}", [128, 128], F32) for l in range(2)]
        y3 = ar.sbuf_tensor("F_y3", [128, CT, QF1], FP16)
        sel = ar.sbuf_tensor("sel_t", [64, 32 * 128], FP16)
        ar.allocate()
        D("sync", sel[:, :], sel_d, writes=["sel"])
        catb = y3
        s1t = zs

        for l in range(2):
            load_w_bf16(wIN[l], f"F_win{l}", dview(inw_d[l]), 4, 2 * DI, stg, "F_stg",
                        seg=512)
            load_w_bf16(wXP[l], f"F_wxp{l}", dview(xpw_d[l]), CT, 64, stg, "F_stg",
                        seg=512)
            load_w_bf16(wMO[l], f"F_wmo{l}", dview(mow_d[l]), CT, DC, stg, "F_stg",
                        seg=512)
            for s0 in range(0, DI, 512):
                D("gpsimd", stg[0:32, 0:512], dtw_d[l][:, s0:s0 + 512], writes=["F_stg"])
                vcopy(wDT[l][:, s0:s0 + 512], stg[0:32, 0:512], reads=["F_stg"],
                      writes=[f"F_wdt{l}"], eng="gpsimd")
            memset(xc[l][:, :, 0:3], 0.0, writes=[f"F_xc{l}"])
            memset(carr[l][:, :], 0.0, writes=[f"F_carr{l}"])
        load_w_bf16(wPO, "F_wpo", dview(pow_d), CT, C, stg, "F_stg", seg=512)

        for ci in range(T // QF1):
            t0 = ci * QF1
            D("sync", sL16[:, :, :], x1_v[:, :, t0:t0 + QF1], writes=["F_sL16"])
            D("sync", srow[0:1, 0:QF1], stats_d[R_MF:R_MF + 1, t0:t0 + QF1], writes=["srow"])
            D("sync", srow[0:1, QF1:2 * QF1], stats_d[R_RFC:R_RFC + 1, t0:t0 + QF1],
              writes=["srow"])
            bcast_row(PS[4][:, 0:QF1], srow[0:1, 0:QF1], writes=["ps4"])
            bcast_row(PS[5][:, 0:QF1], srow[0:1, QF1:2 * QF1], writes=["ps5"])
            for ct in range(CT):
                tt(sL[:, ct, :], sL16[:, ct, :], PS[4][:, 0:QF1], ALU.subtract,
                   reads=["F_sL16", "ps4"], writes=["F_sL"])
                tt(sL[:, ct, :], sL[:, ct, :], PS[5][:, 0:QF1], ALU.mult,
                   reads=["F_sL", "ps5"], writes=["F_sL"])
                ts(sL[:, ct, :], sL[:, ct, :], pcol("gL", ct), pcol("bL", ct),
                   ALU.mult, ALU.add, reads=["F_sL", "pcv"], writes=["F_sL"])
            for j in range(4):
                vcopy(hh[:, j, :], sL[:, j, :], reads=["F_sL"], writes=["F_h"])
            for l in range(2):
                lk = str(l)
                for j in range(4):
                    act(hsq[:, j, :], hh[:, j, :], AF.Square, reads=["F_h"], writes=["F_hsq"])
                    act(hn[:, j, :], hh[:, j, :], AF.Identity, reads=["F_h"], writes=["F_hnb"])
                for j in range(4):
                    mm(pstat[0:1, 0:QF1], onesb[:, :], hsq[:, j, :], j == 0, j == 3,
                       reads=["F_hsq", "onesb"], writes=["pstat"])
                smul(srow[0:1, 0:QF1], pstat[0:1, 0:QF1], 1.0 / DC,
                     reads=["pstat"], writes=["srow"])
                act(srow[0:1, 0:QF1], srow[0:1, 0:QF1], AF.Ln, bias=cbias[0:1, 0:1],
                    reads=["srow", "cbias"], writes=["srow"])
                act(srow[0:1, 0:QF1], srow[0:1, 0:QF1], AF.Exp, scale=-0.5,
                    reads=["srow"], writes=["srow"])
                bcast_row(PS[4][:, 0:QF1], srow[0:1, 0:QF1], writes=["ps4"])
                for j in range(4):
                    tt(hn[:, j, :], hn[:, j, :], PS[4][:, 0:QF1], ALU.mult,
                       reads=["F_hnb", "ps4"], writes=["F_hnb"])
                    ts(hn[:, j, :], hn[:, j, :],
                       pcv[:, PC["rms" + lk] * 8 + j: PC["rms" + lk] * 8 + j + 1], None,
                       ALU.mult, ALU.bypass, reads=["F_hnb", "pcv"], writes=["F_hnb"])
                for ot in range(16):
                    pb = PS[ot % 2]
                    for j in range(4):
                        mm(pb[:, 0:QF1], wIN[l][:, j, ot * 128:(ot + 1) * 128], hn[:, j, :],
                           j == 0, j == 3, reads=[f"F_win{l}", "F_hnb"],
                           writes=[f"psf{ot % 2}"])
                    if ot < 8:
                        vcopy(xc[l][:, ot, 3:QF1 + 3], pb[:, 0:QF1],
                              reads=[f"psf{ot % 2}"], writes=[f"F_xc{l}"])
                    else:
                        act(zs[:, ot - 8, :], pb[:, 0:QF1], AF.Silu,
                            reads=[f"psf{ot % 2}"], writes=["F_zs"])
                for ct in range(CT):
                    cw = lambda j, ct=ct: convw[:, l * 32 + ct * 4 + j: l * 32 + ct * 4 + j + 1]
                    ts(cvt[:, :], xc[l][:, ct, 0:QF1], cw(0), None, ALU.mult, ALU.bypass,
                       reads=[f"F_xc{l}", "convw"], writes=["F_cv"])
                    for j in (1, 2, 3):
                        stt(cvt[:, :], xc[l][:, ct, j:QF1 + j], cw(j), cvt[:, :],
                            ALU.mult, ALU.add, reads=[f"F_xc{l}", "convw", "F_cv"],
                            writes=["F_cv"])
                    act(xcs[:, ct, :], cvt[:, :], AF.Silu,
                        bias=pcv[:, PC["cb" + lk] * 8 + ct: PC["cb" + lk] * 8 + ct + 1],
                        reads=["F_cv", "pcv"], writes=["F_xcs"])
                    vcopy(xc[l][:, ct, 0:3], xc[l][:, ct, QF1:QF1 + 3],
                          reads=[f"F_xc{l}"], writes=[f"F_xc{l}"])
                for ct in range(CT):
                    mm(PS[2][0:64, 0:QF1], wXP[l][:, ct, :], xcs[:, ct, :],
                       ct == 0, ct == CT - 1, reads=[f"F_wxp{l}", "F_xcs"], writes=["ps2"])
                act(dbc[:, :], PS[2][0:64, 0:QF1], AF.Identity, reads=["ps2"], writes=["F_dbc"])
                for ot in range(CT):
                    mm(PS[3][:, 0:QF1], wDT[l][:, ot * 128:(ot + 1) * 128], dbc[0:32, :],
                       True, True, reads=[f"F_wdt{l}", "F_dbc"], writes=["ps3"])
                    act(cvt[:, :], PS[3][:, 0:QF1], AF.Exp,
                        bias=pcv[:, PC["dtb" + lk] * 8 + ot: PC["dtb" + lk] * 8 + ot + 1],
                        reads=["ps3", "pcv"], writes=["F_cv"])
                    act(delt[:, ot, :], cvt[:, :], AF.Ln, bias=cbias[:, 3:4],
                        reads=["F_cv", "cbias"], writes=["F_del"])
                    tt(ut[:, ot, :], delt[:, ot, :], xcs[:, ot, :], ALU.mult,
                       reads=["F_del", "F_xcs"], writes=["F_u"])
                for n in range(NS):
                    pp = n % 2
                    mm(PS[4][:, 0:QF1], sel[:, (2 * n) * 128:(2 * n + 1) * 128], dbc[:, :],
                       True, True, reads=["sel", "F_dbc"], writes=["ps4"])
                    vcopy(bmb[:, 0, :], PS[4][:, 0:QF1], reads=["ps4"], writes=["F_bm"],
                          eng="scalar")
                    mm(PS[5][:, 0:QF1], sel[:, (2 * n + 1) * 128:(2 * n + 2) * 128],
                       dbc[:, :], True, True, reads=["sel", "F_dbc"], writes=["ps5"])
                    vcopy(bmb[:, 1, :], PS[5][:, 0:QF1], reads=["ps5"], writes=["F_cm"],
                          eng="scalar")
                    for ct in range(CT):
                        idx = ct * NS + n
                        act(dA[:, pp, :], delt[:, ct, :], AF.Exp,
                            scale=aneg[:, l * 128 + idx: l * 128 + idx + 1],
                            reads=["F_del", "aneg"], writes=[f"F_dA{pp}"])
                        tt(dBx[:, pp, :], ut[:, ct, :], bmb[:, 0, :], ALU.mult,
                           reads=["F_u", "F_bm"], writes=[f"F_dBx{pp}"])
                        scan(ht[:, pp, :], dA[:, pp, :], dBx[:, pp, :],
                             carr[l][:, idx:idx + 1],
                             reads=[f"F_dA{pp}", f"F_dBx{pp}", f"F_carr{l}"],
                             writes=[f"F_ht{pp}"])
                        vcopy(carr[l][:, idx:idx + 1], ht[:, pp, QF1 - 1:QF1],
                              reads=[f"F_ht{pp}"], writes=[f"F_carr{l}"])
                        tt(dBx[:, pp, :], ht[:, pp, :], bmb[:, 1, :], ALU.mult,
                           reads=[f"F_ht{pp}", "F_cm"], writes=[f"F_dBx{pp}"])
                        if n == 0:
                            vcopy(yac[:, ct, :], dBx[:, pp, :], reads=[f"F_dBx{pp}"],
                                  writes=["F_yac"])
                        else:
                            tt(yac[:, ct, :], yac[:, ct, :], dBx[:, pp, :], ALU.add,
                               reads=["F_yac", f"F_dBx{pp}"], writes=["F_yac"])
                for ct in range(CT):
                    stt(cvt[:, :], xcs[:, ct, :],
                        pcv[:, PC["Dp" + lk] * 8 + ct: PC["Dp" + lk] * 8 + ct + 1],
                        yac[:, ct, :], ALU.mult, ALU.add,
                        reads=["F_xcs", "pcv", "F_yac"], writes=["F_cv"])
                    tt(y3[:, ct, :], cvt[:, :], zs[:, ct, :], ALU.mult,
                       reads=["F_cv", "F_zs"], writes=["F_y3"])
                for ot in range(4):
                    pb = PS[ot % 2]
                    for ct in range(CT):
                        mm(pb[:, 0:QF1], wMO[l][:, ct, ot * 128:(ot + 1) * 128],
                           y3[:, ct, :], ct == 0, ct == CT - 1,
                           reads=[f"F_wmo{l}", "F_y3"], writes=[f"psf{ot % 2}"])
                    tt(hh[:, ot, :], hh[:, ot, :], pb[:, 0:QF1], ALU.add,
                       reads=["F_h", f"psf{ot % 2}"], writes=["F_h"])
            for j in range(4):
                vcopy(catb[:, j, :], sL[:, j, :], reads=["F_sL"], writes=["F_y3"])
                tt(catb[:, 4 + j, :], sL[:, 4 + j, :], hh[:, j, :], ALU.add,
                   reads=["F_sL", "F_h"], writes=["F_y3"])
            for ot in range(CT):
                pb = PS[ot % 2]
                for ct in range(CT):
                    mm(pb[:, 0:QF1], wPO[:, ct, ot * 128:(ot + 1) * 128], catb[:, ct, :],
                       ct == 0, ct == CT - 1, reads=["F_wpo", "F_y3"],
                       writes=[f"psf{ot % 2}"])
                stt(s1t[:, ot, :], pb[:, 0:QF1], pcol("po_b", ot), sL[:, ot, :],
                    ALU.add, ALU.add, reads=[f"psf{ot % 2}", "pcv", "F_sL"],
                    writes=["F_zs"])
            D("sync", s1_v[:, :, t0:t0 + QF1], s1t[:, :, :], reads=["F_zs"])
        sch.set_barrier()
        ar.free()

    if "F2" in _PHASES:
        ar = nc.auto_arena()
        wFC = ar.sbuf_tensor("G_wfc", [128, CT, FF], FP16)
        wPR = ar.sbuf_tensor("G_wpr", [128, FT, C], FP16)
        stg = ar.sbuf_tensor("G_stg", [128, 512], F32)
        rsqb = ar.sbuf_tensor("G_rsq", [128, 2, FT, QF2], FP16)
        x1b = ar.sbuf_tensor("G_x1", [128, 3, CT, QF2], FP16)
        xnb = ar.sbuf_tensor("G_xn", [128, 2, CT, QF2], FP16)
        rl = ar.sbuf_tensor("G_rl", [128, 3, QF2], FP16)
        tf = ar.sbuf_tensor("G_tf", [128, QF2], F32)
        accs = ar.sbuf_tensor("G_acc", [128, 2, QF2], FP16)
        g1 = ar.sbuf_tensor("G_g1", [128, QF2], FP16)
        s1b = ar.sbuf_tensor("G_s1", [128, 2, CT, QF2], FP16)
        tb = ar.sbuf_tensor("G_t", [128, QF2], FP16)
        ot_f = ar.sbuf_tensor("G_out", [128, QF2], F32)
        ar.allocate()
        load_w_bf16(wFC, "G_wfc", dview(wfc_d), CT, FF, stg, "G_stg", seg=512)
        load_w_bf16(wPR, "G_wpr", dview(wpr_d), FT, C, stg, "G_stg", seg=512)
        NCH = T // QF2

        def ybank(ot):
            return PS[2 + ot // 2][:, (ot % 2) * QF2:(ot % 2 + 1) * QF2]

        def g_load(ci):
            p = ci % 2
            p3 = ci % 3
            t0 = ci * QF2
            D("sync", x1b[:, p3, :, :], x1_v[:, :, t0:t0 + QF2], writes=[f"G_x1{p3}"])
            D("sync", srow[0:1, p * 512:p * 512 + QF2],
              stats_d[R_MF:R_MF + 1, t0:t0 + QF2], writes=[f"G_srow{p}"])
            D("sync", srow[0:1, p * 512 + QF2:p * 512 + 2 * QF2],
              stats_d[R_R0F:R_R0F + 1, t0:t0 + QF2], writes=[f"G_srow{p}"])

        def g_load_s1(ci):
            p = ci % 2
            t0 = ci * QF2
            D("sync", s1b[:, p, :, :], s1_v[:, :, t0:t0 + QF2], writes=[f"G_s1{p}"])

        def g_prep(ci):
            p = ci % 2
            p3 = ci % 3
            bcast_row(PS[0][:, 0:QF2], srow[0:1, p * 512:p * 512 + QF2],
                      writes=["ps0"], reads=[f"G_srow{p}"])
            bcast_row(PS[1][:, 0:QF2], srow[0:1, p * 512 + QF2:p * 512 + 2 * QF2],
                      writes=["ps1"], reads=[f"G_srow{p}"])
            for ct in range(CT):
                tt(tf[:, :], x1b[:, p3, ct, :], PS[0][:, 0:QF2],
                   ALU.subtract, reads=[f"G_x1{p3}", "ps0"], writes=["G_tf"])
                tt(xnb[:, p, ct, :], tf[:, :],
                   PS[1][:, 0:QF2], ALU.mult,
                   reads=["G_tf", "ps1"], writes=[f"G_xn{p}"])

        def g_fc(ci):
            p = ci % 2
            for ft in range(FT):
                q = ft % 3
                pb = PS[q] if q < 2 else pstat
                kq = f"ps{q}" if q < 2 else "ps6"
                for ct in range(CT):
                    mm(pb[:, 0:QF2], wFC[:, ct, ft * 128:(ft + 1) * 128],
                       xnb[:, p, ct, :], ct == 0, ct == CT - 1,
                       reads=["G_wfc", f"G_xn{p}"], writes=[kq])
                act(rl[:, q, :], pb[:, 0:QF2], AF.Relu, reads=[kq],
                    writes=[f"G_rl{q}"])
                tt(rsqb[:, p, ft, :], rl[:, q, :], rl[:, q, :], ALU.mult,
                   reads=[f"G_rl{q}"], writes=[f"G_rsq{p}"])

        def g_proj(ci):
            p = ci % 2
            p3 = ci % 3
            t0 = ci * QF2
            for ot in range(CT):
                q = ot % 2
                for ft in range(FT):
                    mm(ybank(ot), wPR[:, ft, ot * 128:(ot + 1) * 128],
                       rsqb[:, p, ft, :], ft == 0, ft == FT - 1,
                       reads=["G_wpr", f"G_rsq{p}"], writes=[f"psy{ot // 2}"])
                ts(accs[:, q, :], ybank(ot), THS[0], None, ALU.is_gt, ALU.bypass,
                   reads=[f"psy{ot // 2}"], writes=[f"G_acc{q}"])
                for th in THS[1:]:
                    stt(accs[:, q, :], ybank(ot), th, accs[:, q, :], ALU.is_gt, ALU.add,
                        reads=[f"psy{ot // 2}", f"G_acc{q}"], writes=[f"G_acc{q}"])
                act(g1[:, :], accs[:, q, :], AF.Silu, scale=0.2,
                    reads=[f"G_acc{q}"], writes=["G_g1"])
                tt(tb[:, :], s1b[:, p, ot, :], g1[:, :], ALU.mult,
                   reads=[f"G_s1{p}", "G_g1"], writes=["G_t"])
                tt(ot_f[:, :], x1b[:, p3, ot, :], tb[:, :], ALU.add,
                   reads=[f"G_x1{p3}", "G_t"], writes=["G_out"])
                D("gpsimd", out_v[:, ot, t0:t0 + QF2], ot_f[:, :],
                  reads=["G_out"])
                if _DBG_G:
                    D("gpsimd", rr_v[:, ot, t0:t0 + QF2], g1[:, :],
                      reads=["G_g1"])

        g_load(0)
        g_prep(0)
        g_load(1)
        g_prep(1)
        g_load_s1(0)
        for ci in range(NCH if _F2_TRUNC != 1 else 0):
            g_fc(ci)
            if _F2_TRUNC != 2 and ci > 0:
                g_proj(ci - 1)
            if ci + 1 < NCH:
                g_load_s1(ci + 1)
            if ci + 2 < NCH:
                g_load(ci + 2)
            if ci + 2 < NCH:
                g_prep(ci + 2)
        if _F2_TRUNC == 0:
            g_proj(NCH - 1)
        ar.free()

    sch.set_barrier()
    E("sync", lambda e: e.nop())

    sch.materialize()
    with nc.Block() as block:
        @block.sync
        def _(eng):
            sch.run_engine("sync", eng)

        @block.vector
        def _(eng):
            sch.run_engine("vector", eng)

        @block.scalar
        def _(eng):
            sch.run_engine("scalar", eng)

        @block.tensor
        def _(eng):
            sch.run_engine("tensor", eng)

        @block.gpsimd
        def _(eng):
            sch.run_engine("gpsimd", eng)
    sch.close()
    for c in reversed(psum_ctx):
        c.__exit__(None, None, None)
    per.free()
    return nc


def _pack_pcvec(inp):
    pc = np.zeros((128, 26 * 8), dtype=np.float32)
    def put(slot_idx, v):
        pc[:, slot_idx * 8:(slot_idx + 1) * 8] = np.asarray(v).reshape(8, 128).T
    put(0, inp["L0_attn_tm_k"]); put(1, inp["L0_attn_tm_v"]); put(2, inp["L0_attn_tm_r"])
    put(3, inp["L0_attn_time_decay"]); put(4, inp["L0_attn_time_first"])
    put(5, inp["L0_attn_ln0_g"]); put(6, inp["L0_attn_ln0_b"])
    put(7, inp["L0_attn_ln1_g"]); put(8, inp["L0_attn_ln1_b"])
    put(9, inp["L0_attn_cm_k"]); put(10, inp["L0_attn_cm_r"])
    put(11, inp["L1_attn_ln_g"]); put(12, inp["L1_attn_ln_b"])
    put(13, inp["L1_attn_po_b"])
    pc[:, 14 * 8:15 * 8] = np.asarray(inp["L1_attn_conv_b"][0]).reshape(8, 128).T
    pc[:, 15 * 8:16 * 8] = np.asarray(inp["L1_attn_conv_b"][1]).reshape(8, 128).T
    pc[:, 16 * 8:17 * 8] = np.asarray(inp["L1_attn_dt_b"][0]).reshape(8, 128).T
    pc[:, 17 * 8:18 * 8] = np.asarray(inp["L1_attn_dt_b"][1]).reshape(8, 128).T
    pc[:, 18 * 8:19 * 8] = np.asarray(inp["L1_attn_Dp"][0]).reshape(8, 128).T
    pc[:, 19 * 8:20 * 8] = np.asarray(inp["L1_attn_Dp"][1]).reshape(8, 128).T
    pc[:, 20 * 8:20 * 8 + 4] = np.asarray(inp["L1_attn_rms_w"][0]).reshape(4, 128).T
    pc[:, 21 * 8:21 * 8 + 4] = np.asarray(inp["L1_attn_rms_w"][1]).reshape(4, 128).T
    return pc


def kernel(**inp):
    nc = _build()
    cc = np.ascontiguousarray

    pcvec = _pack_pcvec(inp)
    alog = cc(np.asarray(inp["L1_attn_A_log"]).reshape(2, 8, 128, 16)
              .transpose(2, 0, 1, 3).reshape(128, 256)).astype(np.float32)
    convw = cc(np.asarray(inp["L1_attn_conv_w"]).reshape(2, 8, 128, 4)
               .transpose(2, 0, 1, 3).reshape(128, 64)).astype(np.float32)
    clips = np.array([np.asarray(inp["L0_ffn_clip_x"])[0],
                      np.asarray(inp["L0_ffn_clip_down"])[0]], dtype=np.float32)
    ident = np.eye(128, dtype=np.float16)
    sel = np.zeros((64, 32 * 128), dtype=np.float16)
    for n in range(16):
        sel[32 + n, (2 * n) * 128:(2 * n + 1) * 128] = 1
        sel[48 + n, (2 * n + 1) * 128:(2 * n + 2) * 128] = 1

    f32 = lambda a: np.asarray(a, dtype=np.float32)
    shared = dict(
        pcvec=pcvec, Alog=alog, convw=convw, clips=clips, ident=ident, sel=sel,
        Wk=cc(f32(inp["L0_attn_Wk"])), Wv=cc(f32(inp["L0_attn_Wv"])),
        Wr=cc(f32(inp["L0_attn_Wr"])), Wo=cc(f32(inp["L0_attn_Wo"])),
        Wck=cc(f32(inp["L0_attn_Wck"])), Wcv=cc(f32(inp["L0_attn_Wcv"])),
        Wcr=cc(f32(inp["L0_attn_Wcr"])),
        wg_n=cc(f32(inp["L0_ffn_w_gate"])), wu_n=cc(f32(inp["L0_ffn_w_up"])),
        wd_n=cc(f32(inp["L0_ffn_w_down"])),
        inwT=cc(f32(inp["L1_attn_in_W"]).transpose(0, 2, 1)),
        xpwT=cc(f32(inp["L1_attn_xp_W"]).transpose(0, 2, 1)),
        dtwT=cc(f32(inp["L1_attn_dt_W"]).transpose(0, 2, 1)),
        mowT=cc(f32(inp["L1_attn_mo_W"]).transpose(0, 2, 1)),
        powT=cc(f32(inp["L1_attn_po_W"]).T),
        wfcT=cc(f32(inp["L1_ffn_W_fc"]).T),
        wprT=cc(f32(inp["L1_ffn_W_proj"]).T),
    )
    x = f32(inp["L0_attn_x"])
    in_maps = []
    for i in range(8):
        m = dict(shared)
        m["x"] = cc(x[i].T)
        in_maps.append(m)
    res = run_bass_kernel_spmd(nc, in_maps, core_ids=list(range(8)))
    outs = [np.asarray(r["out"]) for r in res.results]
    full = np.stack([o.T for o in outs], axis=0)
    if _DEBUG_OUT:
        kernel._dbg = [{k: np.asarray(v) for k, v in r.items() if k != "out"}
                       for r in res.results]
    return np.ascontiguousarray(full.astype(np.float32))
